# version 34
# baseline (speedup 1.0000x reference)
import contextlib
import numpy as np
import concourse.bass as bass
import concourse.mybir as mybir
from concourse.bass_utils import run_bass_kernel_spmd

F32 = mybir.dt.float32
BF16 = mybir.dt.bfloat16
I32 = mybir.dt.int32
AF = mybir.ActivationFunctionType
ALU = mybir.AluOpType

B = 2
L = 8192
DM = 1024
DI = 2048
NST = 16
DTR = 64
HID = 4096
NCORE = 8

C1 = 1024
C2 = 512
NG = 2

LN_EPS = 1e-5
XPW = DTR + 2 * NST


def _dims():
    return B * L, DI // NCORE, B * L // NCORE


class Sem:
    def __init__(self, h):
        self.h = h
        self.n = 0

    def bump(self, k=1):
        self.n += k
        return self.n


class Arena:
    def __init__(self, handle, nbytes):
        self.h = handle
        self.nbytes = nbytes
        self.off = 0
        self.marks = []
        self.peak = 0

    def alloc(self, shape, dtype):
        esz = mybir.dt.size(dtype)
        n = int(np.prod(shape)) * esz
        off = (self.off + 15) & ~15
        assert off + n <= self.nbytes, f"arena overflow: {off}+{n} > {self.nbytes}"
        self.off = off + n
        self.peak = max(self.peak, self.off)
        ap = self.h.ap()[:, off:off + n].bitcast(dtype)
        if len(shape) > 1:
            names = " ".join(f"d{i}" for i in range(len(shape)))
            ap = ap.rearrange(f"p ({names}) -> p {names}",
                              **{f"d{i}": s for i, s in enumerate(shape)})
        return ap

    def push(self):
        self.marks.append(self.off)

    def pop(self):
        self.off = self.marks.pop()


def build_program():
    TTOT, DSL, TSL = _dims()
    DH = DSL // 128
    KDM = DM // 128
    KDI = DI // 128
    KH = HID // 128
    RND = L // C1
    NB1 = C1 // 512
    NTB1 = C1 // 128
    NJ2 = C1 // C2
    SUB = B * C1 // NCORE
    NTB3 = SUB // 128
    NGRP = NST // NG
    assert DSL % 128 == 0 and L % C1 == 0 and C1 % C2 == 0 and NJ2 == 2

    nc = bass.Bass("TRN2", target_bir_lowering=False, debug=False)

    d_x = nc.dram_tensor("x", [TTOT, DM], F32, kind="ExternalInput").ap()
    d_xmine = nc.dram_tensor("xmine", [TSL, DM], F32, kind="ExternalInput").ap()
    d_winT = nc.dram_tensor("winT", [DM, 2 * DSL], F32, kind="ExternalInput").ap()
    d_inb = nc.dram_tensor("inb", [128, 2 * DH], F32, kind="ExternalInput").ap()
    d_convw = nc.dram_tensor("convw", [128, DH * 4], F32, kind="ExternalInput").ap()
    d_convb = nc.dram_tensor("convb", [128, DH], F32, kind="ExternalInput").ap()
    d_xpwT = nc.dram_tensor("xpwT", [128, DH * XPW], F32, kind="ExternalInput").ap()
    d_dtwT = nc.dram_tensor("dtwT", [DTR, DSL], F32, kind="ExternalInput").ap()
    d_dtb = nc.dram_tensor("dtb", [128, DH], F32, kind="ExternalInput").ap()
    d_alog = nc.dram_tensor("alog", [128, DH * NST], F32, kind="ExternalInput").ap()
    d_dpar = nc.dram_tensor("dpar", [128, DH], F32, kind="ExternalInput").ap()
    d_owT = nc.dram_tensor("owT", [DI, DM], F32, kind="ExternalInput").ap()
    d_ob = nc.dram_tensor("ob", [1, DM], F32, kind="ExternalInput").ap()
    d_w1T = nc.dram_tensor("w1T", [HID, DM], F32, kind="ExternalInput").ap()
    d_b1 = nc.dram_tensor("b1", [128, KH], F32, kind="ExternalInput").ap()
    d_w2T = nc.dram_tensor("w2T", [HID, DM], F32, kind="ExternalInput").ap()
    d_b2 = nc.dram_tensor("b2", [1, DM], F32, kind="ExternalInput").ap()
    d_idf = nc.dram_tensor("idf", [128, 128], F32, kind="ExternalInput").ap()
    d_idb = nc.dram_tensor("idb", [128, 128], BF16, kind="ExternalInput").ap()

    d_out = nc.dram_tensor("out", [TSL, DM], F32, kind="ExternalOutput").ap()

    d_uscr = nc.dram_tensor("uscr", [DH, 128, TTOT], BF16, kind="Internal").ap()
    d_gscr = nc.dram_tensor("gscr", [DH, 128, TTOT], BF16, kind="Internal").ap()
    d_xp = nc.dram_tensor("xp", [RND * B * XPW, C1], F32,
                          kind="Internal").ap()
    d_xdbl = nc.dram_tensor("xdbl", [RND * B * XPW, C1], F32,
                            kind="Internal").ap()
    d_yg = nc.dram_tensor("yg", [RND * DI, SUB], BF16, kind="Internal").ap()
    d_ygx = nc.dram_tensor("ygx", [RND * DI, SUB], BF16, kind="Internal").ap()
    d_wq1 = nc.dram_tensor("wq1", [HID, DM], BF16, kind="Internal").ap()
    d_wq2 = nc.dram_tensor("wq2", [HID, DM], BF16, kind="Internal").ap()
    d_wqo = nc.dram_tensor("wqo", [DI, DM], BF16, kind="Internal").ap()
    d_bcb = nc.dram_tensor("bcb", [RND * B * NJ2 * 2 * NST, C2], BF16,
                           kind="Internal").ap()

    ENGS = ("sync", "gpsimd", "scalar", "vector", "tensor")
    prog = {e: [] for e in ENGS}
    engobj = {}

    es = contextlib.ExitStack()
    ARENA_BYTES = 212736
    arena_h = es.enter_context(
        nc.sbuf_tensor("arena", [128, ARENA_BYTES], mybir.dt.int8))
    A = Arena(arena_h, ARENA_BYTES)
    PS = [es.enter_context(nc.psum_tensor(f"ps{i}", [128, 512], F32))
          for i in range(8)]
    PSB = [p.ap().bitcast(BF16) for p in PS]

    def mksem(name):
        return Sem(es.enter_context(nc.semaphore(name)))
    CC = mksem("CC")
    PE = mksem("PE")
    ACT = mksem("ACT")
    VEC = mksem("VEC")
    D_P0L = mksem("D_P0L")
    D_P0S = mksem("D_P0S")
    D_WPL = [mksem("D_WPL0"), mksem("D_WPL1")]
    D_WPS = [mksem("D_WPS0"), mksem("D_WPS1")]
    D_X = [mksem("D_X0"), mksem("D_X1")]
    D_U = mksem("D_U")
    D_G = mksem("D_G")
    D_XPS = mksem("D_XPS")
    D_U2 = mksem("D_U2")
    D_XD = mksem("D_XD")
    D_BR = [mksem("D_BR0"), mksem("D_BR1")]
    D_CR = [mksem("D_CR0"), mksem("D_CR1")]
    D_BC = mksem("D_BC")
    D_YG = mksem("D_YG")
    D_YST = mksem("D_YST")
    D_XR = mksem("D_XR")
    D_W1 = [mksem(f"D_W1{i}") for i in range(4)]
    D_W2 = [mksem(f"D_W2{i}") for i in range(4)]
    D_WO = [mksem("D_WOA"), mksem("D_WOB")]
    D_OUT = mksem("D_OUT")
    ALLSEMS = (CC, PE, ACT, VEC, D_P0L, D_P0S,
               D_WPL[0], D_WPL[1], D_WPS[0], D_WPS[1],
               D_X[0], D_X[1], D_U, D_G,
               D_XPS,
               D_U2, D_XD, D_BR[0], D_BR[1], D_CR[0], D_CR[1], D_BC,
               D_YG, D_YST, D_XR, *D_W1, *D_W2,
               D_WO[0], D_WO[1], D_OUT)

    def step(eng, waits=(), inc=None, inck=1):
        wl = [(s.h, v) for (s, v) in waits if v > 0]
        base = inc.n if inc is not None else None
        if inc is not None:
            inc.bump(inck)

        def deco(fn):
            def run():
                E = engobj[eng]
                for h, v in wl:
                    E.wait_ge(h, v)
                if inc is None:
                    fn(E, None, None)
                    return
                st = {"c": base}

                def link(inst, k=1, wait=True):
                    st["c"] += k
                    assert st["c"] < base + inck, "link() overran inck"
                    inst.then_inc(inc.h, k)
                    if wait:
                        E.wait_ge(inc.h, st["c"])

                def fin(inst):
                    rem = (base + inck) - st["c"]
                    st["c"] = base + inck
                    inst.then_inc(inc.h, rem)
                fn(E, link, fin)
                assert st["c"] == base + inck, "fin() not called"
            prog[eng].append(run)
            return None
        return deco

    def snap():
        return [(s, s.n) for s in ALLSEMS]

    def join(snapshot):
        for e in ENGS:
            @step(e, waits=snapshot)
            def _(E, link, fin):
                pass

    def col(ap2d, i):
        return ap2d[:, i:i + 1]

    bankrd = {i: (VEC, 0) for i in range(8)}

    t_win = A.alloc([KDM, 2 * DSL], BF16)
    t_inb = A.alloc([2 * DH], F32)
    t_xpw = A.alloc([DH, XPW], BF16)
    t_dtw = A.alloc([DSL], BF16)
    t_dtb = A.alloc([DH], F32)
    t_convw = A.alloc([DH, 4], F32)
    t_convb = A.alloc([DH], F32)
    t_a = A.alloc([DH, NST], F32)
    t_dpar = A.alloc([DH], F32)
    t_obrow = A.alloc([DM], F32)
    t_b2row = A.alloc([DM], F32)
    t_b1q16 = A.alloc([KH], F32)
    t_idf = A.alloc([128], F32)
    t_idb = A.alloc([128], BF16)
    t_hst2 = A.alloc([B * DH, NST], F32)
    t_consts = A.alloc([8], F32)
    C_ONE, C_M1, C_M05, C_EPS, C_I1024 = 0, 1, 2, 3, 4

    A.push()
    stg = A.alloc([KDM * 2 * DSL], F32)
    stg2 = A.alloc([HID], F32)
    stgq = A.alloc([HID], F32)
    stgi = A.alloc([HID], I32)
    stgi2 = A.alloc([HID], I32)
    stgb = A.alloc([HID], BF16)

    @step("sync", inc=D_P0L, inck=16 * 10)
    def _(E, link, fin):
        for dst, src in [
            (stg.rearrange("p (a b) -> p a b", a=KDM),
             d_winT.rearrange("(a p) b -> p a b", p=128)),
            (t_inb, d_inb),
            (t_convw, d_convw.rearrange("p (a b) -> p a b", a=DH)),
            (t_convb, d_convb),
            (t_dtb, d_dtb),
            (t_dpar, d_dpar),
            (t_idf, d_idf),
            (t_idb, d_idb),
        ]:
            link(E.dma_start(out=dst, in_=src), 16, wait=False)
        link(E.dma_start(out=t_obrow,
                         in_=d_ob[None, 0, :].to_broadcast([128, DM])),
             16, wait=False)
        fin(E.dma_start(out=t_b2row,
                        in_=d_b2[None, 0, :].to_broadcast([128, DM])))
    ld0 = D_P0L.n

    @step("vector", waits=[(D_P0L, ld0)], inc=VEC)
    def _(E, link, fin):
        for i, v in [(C_ONE, 1.0), (C_M1, -1.0), (C_M05, -0.5),
                     (C_EPS, LN_EPS), (C_I1024, 1.0 / 1024.0)]:
            E.memset(t_consts[:, i:i + 1], v)
        E.memset(t_hst2.rearrange("p a b -> p (a b)"), 0.0)
        fin(E.tensor_copy(t_win.rearrange("p a b -> p (a b)"), stg))
    v_cast_win = VEC.n

    @step("sync", waits=[(VEC, v_cast_win)], inc=D_P0L, inck=16 * 3)
    def _(E, link, fin):
        link(E.dma_start(out=stg[:, :DH * XPW], in_=d_xpwT), 16, wait=False)
        link(E.dma_start(out=stg[0:DTR, DH * XPW:DH * XPW + DSL], in_=d_dtwT),
             16, wait=False)
        fin(E.dma_start(
            out=stg[:, DH * XPW + DSL:DH * XPW + DSL + DH * NST], in_=d_alog))
    ld1 = D_P0L.n

    @step("scalar", waits=[(D_P0L, ld1)], inc=ACT)
    def _(E, link, fin):
        fin(E.activation(t_a.rearrange("p a b -> p (a b)"),
                         stg[:, DH * XPW + DSL:DH * XPW + DSL + DH * NST],
                         AF.Exp))
    a_exp = ACT.n

    @step("vector", waits=[(ACT, a_exp)], inc=VEC)
    def _(E, link, fin):
        a_flat = t_a.rearrange("p a b -> p (a b)")
        E.tensor_scalar_mul(a_flat, a_flat, -1.0)
        E.tensor_copy(t_xpw.rearrange("p a b -> p (a b)"), stg[:, :DH * XPW])
        fin(E.tensor_copy(t_dtw[0:DTR, :],
                          stg[0:DTR, DH * XPW:DH * XPW + DSL]))

    def emit_rne(E, link, src_f32, n, scale):
        link(E.tensor_scalar(stgq[:, :n], src_f32, float(scale), 256.0,
                             ALU.mult, ALU.add))
        link(E.tensor_copy(stgi[:, :n], stgq[:, :n]))
        link(E.tensor_sub(stg2[:, :n], stgq[:, :n], stgi[:, :n]))
        link(E.tensor_single_scalar(stg2[:, :n], stg2[:, :n], 0.5, ALU.is_gt))
        link(E.tensor_add(stgi2[:, :n], stgi[:, :n], stg2[:, :n]))
        link(E.tensor_scalar(stgi2[:, :n], stgi2[:, :n], 256, -128,
                             ALU.subtract, ALU.max))

    @step("sync", waits=[(VEC, VEC.n)], inc=D_P0L, inck=16)
    def _(E, link, fin):
        fin(E.dma_start(out=stg[:, :KH], in_=d_b1))
    ldb1 = D_P0L.n

    @step("vector", waits=[(D_P0L, ldb1)], inc=VEC, inck=8)
    def _(E, link, fin):
        emit_rne(E, link, stg[:, :KH], KH, 64.0)
        link(E.tensor_single_scalar(stgi2[:, :KH], stgi2[:, :KH], 127, ALU.min))
        fin(E.tensor_scalar_mul(t_b1q16, stgi2[:, :KH], 16.0))

    @step("vector", waits=[(VEC, VEC.n)], inc=VEC, inck=8)
    def _(E, link, fin):
        emit_rne(E, link, t_b2row, DM, 64.0)
        link(E.tensor_single_scalar(stgi2[:, :DM], stgi2[:, :DM], 127, ALU.min))
        fin(E.tensor_scalar_mul(t_b2row, stgi2[:, :DM], 1.0 / 64.0))

    w1_src = d_w1T.rearrange("(a p) b -> p a b", p=128)
    wq1_dst = d_wq1.rearrange("(a p) b -> p a b", p=128)
    w2_src = d_w2T.rearrange("(a p) b -> p a b", p=128)
    wq2_dst = d_wq2.rearrange("(a p) b -> p a b", p=128)
    wo_src = d_owT.rearrange("(a p) b -> p a b", p=128)
    wqo_dst = d_wqo.rearrange("(a p) b -> p a b", p=128)
    A.pop()

    wprep_done = {}

    def emit_WPREP():
        units = []
        for kind, src, dst, ntile, quant in [
                ("wo", wo_src, wqo_dst, KDI, False),
                ("w1", w1_src, wq1_dst, KH, True),
                ("w2", w2_src, wq2_dst, KH, True)]:
            for t in range(ntile):
                for h in range(2):
                    units.append((kind, src, dst, t, h, quant))
        for u, (kind, src, dst, t, h, quant) in enumerate(units):
            sl = u % 2

            @step("sync", waits=[(D_WPS[sl], D_WPS[sl].n)],
                  inc=D_WPL[sl], inck=16)
            def _(E, link, fin, src=src, t=t, h=h, sl=sl):
                fin(E.dma_start(out=t_ws[sl],
                                in_=src[:, t, h * 512:(h + 1) * 512]))
            ldw = D_WPL[sl].n

            if quant:
                @step("vector", waits=[(D_WPL[sl], ldw),
                                       (D_WPS[sl], D_WPS[sl].n)],
                      inc=VEC, inck=7)
                def _(E, link, fin, sl=sl):
                    link(E.tensor_scalar(t_wq, t_ws[sl], 64.0, 256.0,
                                         ALU.mult, ALU.add))
                    link(E.tensor_copy(t_wi, t_wq))
                    link(E.tensor_sub(t_wq, t_wq, t_wi))
                    link(E.tensor_single_scalar(t_wq, t_wq, 0.5, ALU.is_gt))
                    link(E.tensor_add(t_wi2, t_wi, t_wq))
                    link(E.tensor_scalar(t_wi2, t_wi2, 256, -128,
                                         ALU.subtract, ALU.max))
                    fin(E.tensor_single_scalar(t_wb[sl], t_wi2, 127, ALU.min))
            else:
                @step("vector", waits=[(D_WPL[sl], ldw),
                                       (D_WPS[sl], D_WPS[sl].n)],
                      inc=VEC)
                def _(E, link, fin, sl=sl):
                    fin(E.tensor_copy(t_wb[sl], t_ws[sl]))
            vq = VEC.n

            @step("scalar", waits=[(VEC, vq)], inc=D_WPS[sl], inck=16)
            def _(E, link, fin, dst=dst, t=t, h=h, sl=sl):
                fin(E.dma_start(out=dst[:, t, h * 512:(h + 1) * 512],
                                in_=t_wb[sl]))
            wprep_done.setdefault((kind, t), []).append(
                (D_WPS[sl], D_WPS[sl].n))
            yield
    join(snap())

    t_xstg = [A.alloc([DM], F32) for _ in range(2)]
    t_xT = A.alloc([KDM, C1], BF16)
    t_xbraw = A.alloc([DH, C1 + 4], F32)
    t_tail = A.alloc([B, DH, 4], F32)
    t_acc_a = A.alloc([C1], F32)
    t_acc_b = A.alloc([C1], F32)
    t_u1b = A.alloc([DH, C1], BF16)
    t_g1 = A.alloc([DH, C1], BF16)
    t_xp1 = t_acc_a
    t_u2 = [A.alloc([DH, C2], BF16) for _ in range(2)]
    t_g2 = [A.alloc([DH, C2], BF16) for _ in range(2)]
    _xds = A.alloc([C2], F32)
    t_xd = [_xds, _xds]
    t_xdb = A.alloc([C2], BF16)
    t_z2 = A.alloc([DH, C2], F32)
    t_az = A.alloc([DH, C2], F32)
    t_delta = A.alloc([DH, C2], F32)
    t_du = A.alloc([DH, C2], BF16)
    t_brep = [A.alloc([NG, C2], BF16) for _ in range(2)]
    t_crep = [A.alloc([NG, C2], BF16) for _ in range(2)]
    t_dA = [A.alloc([NG, C2], F32) for _ in range(3)]
    t_dBu = A.alloc([NG, C2], BF16)
    t_h2 = A.alloc([NG, C2], BF16)
    t_y2 = t_z2
    t_yg2 = A.alloc([DH, C2], BF16)
    t_yT = A.alloc([KDI, SUB], BF16)
    t_wos = [A.alloc([DM], BF16) for _ in range(2)]
    t_v3 = A.alloc([DM], F32)
    t_xres = A.alloc([DM], F32)
    t_scr = t_v3
    t_stats = A.alloc([8], F32)
    S_S1, S_S2, S_MU, S_VAR, S_RSTD = 0, 1, 2, 3, 4
    t_ln1save = [A.alloc([NTB3, DM], F32) for _ in range(2)]
    t_qi = A.alloc([DM], I32)
    t_xqb = A.alloc([DM], BF16)
    t_x1T = A.alloc([KDM, SUB], BF16)
    t_w1s = [A.alloc([KDM, 128], BF16) for _ in range(4)]
    t_h1q = A.alloc([KH, SUB], BF16)
    t_w2s = [A.alloc([DM], BF16) for _ in range(4)]
    t_ffn = A.alloc([DM], F32)
    t_ws = [A.alloc([512], F32) for _ in range(2)]
    t_wq = A.alloc([512], F32)
    t_wi = t_qi[:, 0:512]
    t_wi2 = t_qi[:, 512:1024]
    t_wb = [A.alloc([512], BF16) for _ in range(2)]
    t_qf = t_ffn
    t_mi = t_qi[:, 0:SUB]
    t_mb = t_qi[:, SUB:2 * SUB]
    t_ms = t_mb
    assert A.peak <= A.nbytes, A.peak

    lc = {}

    def lcw(key):
        return lc.get(key, (VEC, 0))

    cc1_done = {}
    cc2_done = {}
    xp_stored = {}
    ug_stored = {}
    yg_stored = {}

    def emit_A1a(bt, r):
        tok0 = bt * L + r * C1
        ldx_of = {}

        def xload(tb):
            xp_par = tb % 2

            @step("scalar", waits=[lcw(("xstg", xp_par))], inc=D_X[xp_par],
                  inck=16)
            def _(E, link, fin, tok0=tok0, tb=tb, xp_par=xp_par):
                fin(E.dma_start(
                    out=t_xstg[xp_par],
                    in_=d_x[tok0 + tb * 128: tok0 + (tb + 1) * 128, :]))
            ldx_of[tb] = D_X[xp_par].n

        def xtrans(tb):
            xp_par = tb % 2
            ldx = ldx_of[tb]
            for g in range(2):
                bank = 4 + g
                @step("tensor", waits=[(D_X[xp_par], ldx), bankrd[bank]],
                      inc=PE)
                def _(E, link, fin, g=g, bank=bank, xp_par=xp_par):
                    for j in range(4):
                        kb = g * 4 + j
                        inst = E.transpose(
                            PS[bank][:, j * 128:(j + 1) * 128],
                            t_xstg[xp_par][:, kb * 128:(kb + 1) * 128],
                            t_idf)
                    fin(inst)
                pet = PE.n
                lc[("xstg", xp_par)] = (PE, PE.n)

                @step("scalar", waits=[(PE, pet)], inc=ACT)
                def _(E, link, fin, g=g, tb=tb, bank=bank):
                    fin(E.copy(
                        t_xT[:, g * 4:(g + 1) * 4, tb * 128:(tb + 1) * 128],
                        PS[bank][:, :512]
                        .rearrange("p (a b) -> p a b", a=4)))
                bankrd[bank] = (ACT, ACT.n)

        for it in range(NTB1 + 2):
            if it >= 2:
                xtrans(it - 2)
            if it < NTB1:
                xload(it)
            if it >= 2:
                yield
        act_xt = ACT.n

        for s in range(NB1):
            for pb in range(4):
                bank = 6 + (pb % 2)
                @step("tensor", waits=[(ACT, act_xt), bankrd[bank]],
                      inc=PE)
                def _(E, link, fin, s=s, pb=pb, bank=bank):
                    for kb in range(KDM):
                        inst = E.matmul(
                            PS[bank][:, :512],
                            t_win[:, kb, pb * 128:(pb + 1) * 128],
                            t_xT[:, kb, s * 512:(s + 1) * 512],
                            start=(kb == 0), stop=(kb == KDM - 1))
                    fin(inst)
                pei = PE.n

                @step("scalar", waits=[(PE, pei), lcw(("a1evac", pb))],
                      inc=ACT)
                def _(E, link, fin, s=s, pb=pb, bank=bank):
                    if pb < 2:
                        fin(E.activation(
                            t_xbraw[:, pb, 3 + s * 512: 3 + (s + 1) * 512],
                            PS[bank][:, :512], AF.Identity,
                            bias=col(t_inb, pb)))
                    else:
                        fin(E.activation(
                            t_g1[:, pb - 2, s * 512:(s + 1) * 512],
                            PS[bank][:, :512], AF.Silu,
                            bias=col(t_inb, DH + pb - 2)))
                bankrd[bank] = (ACT, ACT.n)
                yield

    def emit_A1b(bt, r):
        tok0 = bt * L + r * C1
        act_inproj = ACT.n
        for dh in range(DH):
            @step("vector",
                  waits=[(ACT, act_inproj), (VEC, VEC.n),
                         (D_XPS, D_XPS.n), lcw(("accb",)), lcw(("acca",))],
                  inc=VEC, inck=7)
            def _(E, link, fin, dh=dh, bt=bt, r=r):
                src = t_xbraw[:, dh, :]
                if r == 0:
                    link(E.memset(src[:, 0:3], 0.0))
                else:
                    link(E.tensor_copy(src[:, 0:3], t_tail[:, bt, dh, 0:3]))
                link(E.tensor_scalar(t_acc_a, src[:, 0:C1],
                                     col(t_convw[:, dh, :], 0), None,
                                     ALU.mult))
                link(E.scalar_tensor_tensor(t_acc_b, src[:, 1:C1 + 1],
                                            col(t_convw[:, dh, :], 1),
                                            t_acc_a, ALU.mult, ALU.add))
                link(E.scalar_tensor_tensor(t_acc_a, src[:, 2:C1 + 2],
                                            col(t_convw[:, dh, :], 2),
                                            t_acc_b, ALU.mult, ALU.add))
                link(E.scalar_tensor_tensor(t_acc_b, src[:, 3:C1 + 3],
                                            col(t_convw[:, dh, :], 3),
                                            t_acc_a, ALU.mult, ALU.add))
                fin(E.tensor_copy(t_tail[:, bt, dh, 0:3], src[:, C1:C1 + 3]))
            vconv = VEC.n
            lc[("a1evac", dh)] = (VEC, vconv)

            @step("scalar", waits=[(VEC, vconv), lcw(("u1b",)),
                                   (D_U, D_U.n)],
                  inc=ACT)
            def _(E, link, fin, dh=dh):
                fin(E.activation(t_u1b[:, dh, :], t_acc_b, AF.Silu,
                                 bias=col(t_convb, dh)))
            lc[("accb",)] = (ACT, ACT.n)
            yield
        act_u = ACT.n

        for s in range(NB1):
            bank = 6
            @step("tensor", waits=[(ACT, act_u), bankrd[bank]], inc=PE)
            def _(E, link, fin, s=s, bank=bank):
                for dh in range(DH):
                    inst = E.matmul(PS[bank][0:XPW, :512],
                                    t_xpw[:, dh, :],
                                    t_u1b[:, dh, s * 512:(s + 1) * 512],
                                    start=(dh == 0), stop=(dh == DH - 1))
                fin(inst)
            pex = PE.n

            @step("scalar", waits=[(PE, pex), lcw(("xp1",))], inc=ACT)
            def _(E, link, fin, s=s, bank=bank):
                fin(E.copy(t_xp1[0:XPW, s * 512:(s + 1) * 512],
                           PS[bank][0:XPW, :512]))
            bankrd[bank] = (ACT, ACT.n)
            yield
        act_xp = ACT.n
        lc[("u1b",)] = (PE, PE.n)
        lc[("acca",)] = (ACT, act_xp)

        @step("sync", waits=[(ACT, act_xp)], inc=D_XPS, inck=16)
        def _(E, link, fin, r=r, bt=bt):
            rb0 = (r * B + bt) * XPW
            fin(E.dma_start(out=d_xp[rb0:rb0 + XPW, :],
                            in_=t_xp1[0:XPW, :]))
        lc[("xp1",)] = (D_XPS, D_XPS.n)

        @step("sync", inc=D_U, inck=32)
        def _(E, link, fin, tok0=tok0):
            link(E.dma_start(
                out=d_uscr[:, :, tok0:tok0 + C1].rearrange("a p c -> p a c"),
                in_=t_u1b), 16, wait=False)
            fin(E.dma_start(
                out=d_gscr[:, :, tok0:tok0 + C1].rearrange("a p c -> p a c"),
                in_=t_g1))
        lc[("a1evac", 2)] = (D_U, D_U.n)
        lc[("a1evac", 3)] = (D_U, D_U.n)
        ug_stored[(bt, r)] = (D_U, D_U.n)
        xp_stored[(bt, r)] = (D_XPS, D_XPS.n)

        @step("gpsimd", waits=[xp_stored[(bt, r)]], inc=CC)
        def _(E, link, fin, r=r, bt=bt):
            rb0 = (r * B + bt) * XPW
            fin(E.collective_compute(
                "AllReduce", ALU.add, replica_groups=[list(range(NCORE))],
                ins=[d_xp[rb0:rb0 + XPW, :]],
                outs=[d_xdbl[rb0:rb0 + XPW, :]]))
        cc1_done[(r, bt)] = CC.n
        yield


    def emit_A2(bt, r, j, P):
        tok0 = bt * L + r * C1 + j * C2
        cc1v = cc1_done[(r, bt)]

        @step("gpsimd",
              waits=[ug_stored[(bt, r)], lcw(("u2", P)),
                     lcw(("g2", P))],
              inc=D_U2, inck=32)
        def _(E, link, fin, tok0=tok0, P=P):
            link(E.dma_start(out=t_u2[P], in_=d_uscr[:, :, tok0:tok0 + C2]
                             .rearrange("a p c -> p a c")), 16, wait=False)
            fin(E.dma_start(out=t_g2[P], in_=d_gscr[:, :, tok0:tok0 + C2]
                            .rearrange("a p c -> p a c")))
        ldug = D_U2.n

        @step("gpsimd", waits=[(CC, cc1v), lcw(("xd", P))], inc=D_XD,
              inck=16)
        def _(E, link, fin, r=r, bt=bt, j=j, P=P):
            rb0 = (r * B + bt) * XPW
            fin(E.dma_start(out=t_xd[P][0:XPW, :],
                            in_=d_xdbl[rb0:rb0 + XPW,
                                       j * C2:(j + 1) * C2]))
        ldxd = D_XD.n

        @step("vector", waits=[(D_XD, ldxd), (VEC, VEC.n), lcw(("xdb",)),
                               lcw(("xdbst",))],
              inc=VEC)
        def _(E, link, fin, P=P):
            fin(E.tensor_copy(t_xdb[0:XPW, :], t_xd[P][0:XPW, :]))
        vxdb = VEC.n
        lc[("xd", P)] = (VEC, vxdb)

        bcrow0 = ((r * B + bt) * NJ2 + j) * 2 * NST

        @step("gpsimd", waits=[(VEC, vxdb)], inc=D_BC, inck=16)
        def _(E, link, fin, bcrow0=bcrow0):
            fin(E.dma_start(out=d_bcb[bcrow0:bcrow0 + 2 * NST, :],
                            in_=t_xdb[DTR:XPW, :]))
        lc[("xdbst",)] = (D_BC, D_BC.n)
        bcv = D_BC.n
        yield

        @step("tensor", waits=[(VEC, vxdb), bankrd[4], bankrd[5]], inc=PE)
        def _(E, link, fin):
            for pb in range(DH):
                inst = E.matmul(PS[4 + pb][:, :C2],
                                t_dtw[0:DTR, pb * 128:(pb + 1) * 128],
                                t_xdb[0:DTR, :],
                                start=True, stop=True)
            fin(inst)
        pedt = PE.n
        lc[("xdb",)] = (PE, pedt)

        @step("scalar", waits=[(PE, pedt), lcw(("z2",)), lcw(("az",))],
              inc=ACT, inck=4)
        def _(E, link, fin):
            for pb in range(DH):
                link(E.activation(t_z2[:, pb, :], PS[4 + pb][:, :C2],
                                  AF.Identity, bias=col(t_dtb, pb)))
            for pb in range(DH):
                inst = E.activation(t_az[:, pb, :], PS[4 + pb][:, :C2],
                                    AF.Abs, bias=col(t_dtb, pb))
                if pb < DH - 1:
                    link(inst)
            fin(inst)
        bankrd[4] = (ACT, ACT.n)
        bankrd[5] = (ACT, ACT.n)

        @step("scalar", waits=[(ACT, ACT.n)], inc=ACT, inck=2)
        def _(E, link, fin):
            azf = t_az.rearrange("p a b -> p (a b)")
            link(E.activation(azf, azf, AF.Exp, scale=-1.0))
            fin(E.activation(azf, azf, AF.Ln, bias=col(t_consts, C_ONE)))
        actsp = ACT.n
        yield

        for dh in range(DH):
            @step("vector",
                  waits=[(ACT, actsp), (D_U2, ldug), lcw(("dadu", dh))],
                  inc=VEC, inck=2)
            def _(E, link, fin, dh=dh):
                link(E.scalar_tensor_tensor(t_delta[:, dh, :],
                                            t_z2[:, dh, :], 0.0,
                                            t_az[:, dh, :],
                                            ALU.max, ALU.add))
                fin(E.tensor_mul(t_du[:, dh, :], t_delta[:, dh, :],
                                 t_u2[P][:, dh, :]))
            lc[("az",)] = (VEC, VEC.n)
            yield
        vdu = VEC.n

        groups = [(g, dh) for g in range(NGRP) for dh in range(DH)]
        NGT = len(groups)
        ld_br = {}
        ld_cr = {}

        def emit_load(i):
            g = groups[i][0]
            gp = g % 2

            @step("scalar", waits=[(D_BC, bcv), lcw(("brep", gp))],
                  inc=D_BR[gp], inck=16)
            def _(E, link, fin, g=g, gp=gp, bcrow0=bcrow0):
                r0 = bcrow0 + g * NG
                fin(E.dma_start(
                    out=t_brep[gp],
                    in_=d_bcb[None, r0:r0 + NG, :]
                    .to_broadcast([128, NG, C2])))
            ld_br[g] = D_BR[gp].n

            @step("gpsimd", waits=[(D_BC, bcv), lcw(("crep", gp))],
                  inc=D_CR[gp], inck=16)
            def _(E, link, fin, g=g, gp=gp, bcrow0=bcrow0):
                r0 = bcrow0 + NST + g * NG
                fin(E.dma_start(
                    out=t_crep[gp],
                    in_=d_bcb[None, r0:r0 + NG, :]
                    .to_broadcast([128, NG, C2])))
            ld_cr[g] = D_CR[gp].n

        def emit_exp(i):
            g, dh = groups[i]
            ip = i % 3

            @step("scalar", waits=[(VEC, vdu), lcw(("dA", ip))], inc=ACT)
            def _(E, link, fin, dh=dh, g=g, ip=ip):
                for n in range(NG):
                    inst = E.activation(t_dA[ip][:, n, :],
                                        t_delta[:, dh, :], AF.Exp,
                                        scale=t_a[:, dh, g * NG + n:
                                                  g * NG + n + 1])
                fin(inst)
            return ACT.n

        actda_of = {}

        def emit_scan(i):
            g, dh = groups[i]
            gp = g % 2
            ip = i % 3
            actda = actda_of[i]

            @step("vector",
                  waits=[(ACT, actda), (D_BR[gp], ld_br[g]),
                         (D_CR[gp], ld_cr[g]), (VEC, VEC.n)],
                  inc=VEC, inck=6)
            def _(E, link, fin, dh=dh, g=g, gp=gp, ip=ip, bt=bt):
                link(E.tensor_mul(
                    t_dBu,
                    t_du[:, dh, :][:, None, :].to_broadcast([128, NG, C2]),
                    t_brep[gp]))
                for n in range(NG):
                    inst = E.tensor_tensor_scan(
                        t_h2[:, n, :], t_dA[ip][:, n, :], t_dBu[:, n, :],
                        t_hst2[:, bt * DH + dh, g * NG + n:g * NG + n + 1],
                        ALU.mult, ALU.add)
                link(inst)
                link(E.tensor_copy(
                    t_hst2[:, bt * DH + dh, g * NG:(g + 1) * NG],
                    t_h2[:, :, C2 - 1:C2].rearrange("p a b -> p (a b)")))
                link(E.tensor_mul(t_dBu, t_h2, t_crep[gp]))
                if g == 0:
                    fin(E.tensor_add(t_y2[:, dh, :], t_dBu[:, 0, :],
                                     t_dBu[:, 1, :]))
                else:
                    link(E.tensor_add(t_y2[:, dh, :], t_y2[:, dh, :],
                                      t_dBu[:, 0, :]))
                    fin(E.tensor_add(t_y2[:, dh, :], t_y2[:, dh, :],
                                     t_dBu[:, 1, :]))
            lc[("dA", ip)] = (VEC, VEC.n)
            if dh == DH - 1:
                lc[("brep", gp)] = (VEC, VEC.n)
                lc[("crep", gp)] = (VEC, VEC.n)
            lc[("dadu", dh)] = (VEC, VEC.n)
            if g == NGRP - 1:
                @step("vector", waits=[(VEC, VEC.n), lcw(("yg2",))],
                      inc=VEC, inck=2)
                def _(E, link, fin, dh=dh, P=P):
                    link(E.scalar_tensor_tensor(t_y2[:, dh, :],
                                                t_u2[P][:, dh, :],
                                                col(t_dpar, dh),
                                                t_y2[:, dh, :],
                                                ALU.mult, ALU.add))
                    fin(E.tensor_mul(t_yg2[:, dh, :], t_y2[:, dh, :],
                                     t_g2[P][:, dh, :]))
                lc[("z2",)] = (VEC, VEC.n)

        for i in range(NGT + 2):
            if i >= 2:
                emit_scan(i - 2)
            if i < NGT:
                actda_of[i] = emit_exp(i)
            if i < NGT and groups[i][1] == 0:
                emit_load(i)
            if i >= 2:
                yield
        vyg = VEC.n
        lc[("u2", P)] = (VEC, vyg)
        lc[("g2", P)] = (VEC, vyg)

        NDD = C2 // SUB

        @step("sync", waits=[(VEC, vyg)], inc=D_YG, inck=16 * NDD)
        def _(E, link, fin, r=r, bt=bt, j=j):
            dest0 = (bt * C1 + j * C2) // SUB
            for dd in range(NDD):
                row0 = r * DI + (dest0 + dd) * DSL
                inst = E.dma_start(
                    out=d_yg[row0:row0 + DSL, :]
                    .rearrange("(a p) c -> p a c", p=128),
                    in_=t_yg2[:, :, dd * SUB:(dd + 1) * SUB])
                if dd < NDD - 1:
                    link(inst, 16, wait=False)
            fin(inst)
        lc[("yg2",)] = (D_YG, D_YG.n)
        yg_stored[(bt, r, j)] = (D_YG, D_YG.n)

    def emit_A3a(r):
        cc2v = cc2_done[r]
        @step("sync", waits=[(CC, cc2v), lcw(("yT",))],
              inc=D_YST, inck=16)
        def _(E, link, fin, r=r):
            fin(E.dma_start(
                out=t_yT,
                in_=d_ygx[r * DI:(r + 1) * DI, :]
                .rearrange("(a p) c -> p a c", p=128)))
        ldyT = D_YST.n
        yield

        for kb in range(KDI):
            wpar = kb % 2
            prev = lc.get(("wo", wpar), (VEC, 0))
            @step("sync", waits=[prev] + wprep_done[("wo", kb)],
                  inc=D_WO[wpar], inck=16)
            def _(E, link, fin, kb=kb, wpar=wpar):
                fin(E.dma_start(out=t_wos[wpar],
                                in_=wqo_dst[:, kb, :]))
            ldwo = D_WO[wpar].n

            waits = [(D_YST, ldyT), (D_WO[wpar], ldwo)]
            if kb == 0:
                waits += [bankrd[i] for i in range(4)]
            @step("tensor", waits=waits, inc=PE)
            def _(E, link, fin, kb=kb, wpar=wpar):
                for tb in range(NTB3):
                    for half in range(2):
                        inst = E.matmul(
                            PS[tb * 2 + half][:, :512],
                            t_yT[:, kb, tb * 128:(tb + 1) * 128],
                            t_wos[wpar][:, half * 512:(half + 1) * 512],
                            start=(kb == 0), stop=(kb == KDI - 1))
                fin(inst)
            lc[("wo", wpar)] = (PE, PE.n)
            yield
        peop = PE.n
        lc[("yT",)] = (PE, peop)

        for tb in range(NTB3):
            @step("sync", waits=[lcw(("xres",))], inc=D_XR, inck=16)
            def _(E, link, fin, r=r, tb=tb):
                r0 = r * SUB + tb * 128
                fin(E.dma_start(out=t_xres, in_=d_xmine[r0:r0 + 128, :]))
            ldr = D_XR.n

            @step("vector", waits=[(PE, peop), (D_XR, ldr), (VEC, VEC.n),
                                   (D_OUT, D_OUT.n), lcw(("v3",))],
                  inc=VEC, inck=3)
            def _(E, link, fin, tb=tb):
                link(E.tensor_add(t_v3[:, 0:512], PS[tb * 2][:, :512],
                                  t_obrow[:, 0:512]))
                link(E.tensor_add(t_v3[:, 512:1024], PS[tb * 2 + 1][:, :512],
                                  t_obrow[:, 512:1024]))
                fin(E.tensor_add(t_v3, t_v3, t_xres))
            vv = VEC.n
            bankrd[tb * 2] = (VEC, vv)
            bankrd[tb * 2 + 1] = (VEC, vv)
            lc[("xres",)] = (VEC, vv)

            @step("scalar", waits=[(VEC, vv), (D_OUT, D_OUT.n)],
                  inc=ACT, inck=2)
            def _(E, link, fin):
                link(E.activation(t_ffn, t_v3, AF.Copy,
                                  accum_out=col(t_stats, S_S1)))
                fin(E.activation(t_ffn, t_v3, AF.Square,
                                 accum_out=col(t_stats, S_S2)))
            actst = ACT.n

            @step("vector", waits=[(ACT, actst)], inc=VEC, inck=3)
            def _(E, link, fin):
                link(E.tensor_scalar_mul(col(t_stats, S_MU),
                                         col(t_stats, S_S1), 1.0 / DM))
                link(E.tensor_mul(col(t_stats, S_VAR), col(t_stats, S_MU),
                                  col(t_stats, S_MU)))
                fin(E.scalar_tensor_tensor(col(t_stats, S_VAR),
                                           col(t_stats, S_S2), 1.0 / DM,
                                           col(t_stats, S_VAR),
                                           ALU.mult, ALU.subtract))
            vvar = VEC.n

            @step("scalar", waits=[(VEC, vvar)], inc=ACT, inck=2)
            def _(E, link, fin):
                link(E.activation(col(t_stats, S_RSTD), col(t_stats, S_VAR),
                                  AF.Ln, bias=col(t_consts, C_EPS)))
                fin(E.activation(col(t_stats, S_RSTD), col(t_stats, S_RSTD),
                                 AF.Exp, scale=col(t_consts, C_M05)))
            actrs = ACT.n

            @step("vector", waits=[(ACT, actrs)], inc=VEC, inck=9)
            def _(E, link, fin, tb=tb):
                link(E.tensor_scalar(t_scr, t_v3, col(t_stats, S_MU),
                                     col(t_stats, S_RSTD), ALU.subtract,
                                     ALU.mult))
                link(E.tensor_copy(t_ln1save[r % 2][:, tb, :], t_scr))
                link(E.tensor_scalar(t_qf, t_scr, 16.0, 256.0, ALU.mult,
                                     ALU.add))
                link(E.tensor_copy(t_qi, t_qf))
                link(E.tensor_sub(t_qf, t_qf, t_qi))
                link(E.tensor_single_scalar(t_qf, t_qf, 0.5, ALU.is_gt))
                link(E.tensor_add(t_qi, t_qi, t_qf))
                link(E.tensor_scalar(t_qi, t_qi, 256, -128, ALU.subtract,
                                     ALU.max))
                fin(E.tensor_single_scalar(t_xqb, t_qi, 127, ALU.min))
            vxq = VEC.n
            lc[("v3",)] = (VEC, vxq)
            yield

            for g in range(2):
                bank = 4 + g
                @step("tensor", waits=[(VEC, vxq), bankrd[bank]], inc=PE)
                def _(E, link, fin, g=g, bank=bank):
                    for j in range(4):
                        kb = g * 4 + j
                        inst = E.transpose(
                            PSB[bank][:, j * 128:(j + 1) * 128],
                            t_xqb[:, kb * 128:(kb + 1) * 128], t_idb)
                    fin(inst)
                pexq = PE.n

                @step("scalar", waits=[(PE, pexq)], inc=ACT)
                def _(E, link, fin, g=g, tb=tb, bank=bank):
                    fin(E.copy(
                        t_x1T[:, g * 4:(g + 1) * 4,
                              tb * 128:(tb + 1) * 128],
                        PSB[bank][:, :512]
                        .rearrange("p (a b) -> p a b", a=4)))
                bankrd[bank] = (ACT, ACT.n)
                lc[("x1T",)] = (ACT, ACT.n)
                yield

    def emit_A3b(r):
        act_x1t = lc.get(("x1T",), (ACT, 0))[1]
        for hb in range(KH):
            wpar = hb % 4
            bank = 6 + hb % 2
            prev = lc.get(("w1", wpar), (VEC, 0))
            @step("sync", waits=[prev] + wprep_done[("w1", hb)],
                  inc=D_W1[wpar], inck=16)
            def _(E, link, fin, hb=hb, wpar=wpar):
                src = d_wq1[hb * 128:(hb + 1) * 128, :]
                fin(E.dma_start(
                    out=t_w1s[wpar].rearrange("p a b -> p (a b)"), in_=src))
            ldw1 = D_W1[wpar].n

            @step("tensor",
                  waits=[(ACT, act_x1t), (D_W1[wpar], ldw1), bankrd[bank]],
                  inc=PE)
            def _(E, link, fin, hb=hb, wpar=wpar, bank=bank):
                for kb in range(KDM):
                    inst = E.matmul(PS[bank][:, :SUB],
                                    t_w1s[wpar][:, kb, :],
                                    t_x1T[:, kb, :],
                                    start=(kb == 0), stop=(kb == KDM - 1))
                fin(inst)
            lc[("w1", wpar)] = (PE, PE.n)
            peh = PE.n

            @step("vector", waits=[(PE, peh), (VEC, VEC.n), lcw(("h1q",))],
                  inc=VEC, inck=6)
            def _(E, link, fin, hb=hb, bank=bank):
                link(E.tensor_scalar(t_mi, PS[bank][:, :SUB],
                                     col(t_b1q16, hb), None, ALU.add))
                link(E.tensor_single_scalar(t_mb, t_mi, 6,
                                            ALU.arith_shift_right))
                link(E.tensor_single_scalar(t_mb, t_mb, 1, ALU.bitwise_and))
                link(E.scalar_tensor_tensor(t_ms, t_mi, 31, t_mb, ALU.add,
                                            ALU.add))
                link(E.tensor_single_scalar(t_ms, t_ms, 6,
                                            ALU.arith_shift_right))
                fin(E.tensor_scalar(t_h1q[:, hb, :], t_ms, 0, 127,
                                    ALU.max, ALU.min))
            bankrd[bank] = (VEC, VEC.n)
            yield
        v_h1 = VEC.n

    def emit_A3c(r):
        v_h1 = VEC.n
        for kb in range(KH):
            wpar = kb % 4
            prev = lc.get(("w2", wpar), (VEC, 0))
            @step("sync", waits=[prev] + wprep_done[("w2", kb)],
                  inc=D_W2[wpar], inck=16)
            def _(E, link, fin, kb=kb, wpar=wpar):
                src = d_wq2.rearrange("(a p) b -> p a b", p=128)[:, kb, :]
                fin(E.dma_start(out=t_w2s[wpar], in_=src))
            ldw2 = D_W2[wpar].n

            waits = [(VEC, v_h1), (D_W2[wpar], ldw2)]
            if kb == 0:
                waits += [bankrd[i] for i in range(4)]
            @step("tensor", waits=waits, inc=PE)
            def _(E, link, fin, kb=kb, wpar=wpar):
                for tb in range(NTB3):
                    for half in range(2):
                        inst = E.matmul(
                            PS[tb * 2 + half][:, :512],
                            t_h1q[:, kb, tb * 128:(tb + 1) * 128],
                            t_w2s[wpar][:, half * 512:(half + 1) * 512],
                            start=(kb == 0), stop=(kb == KH - 1))
                fin(inst)
            lc[("w2", wpar)] = (PE, PE.n)
            yield
        pef2 = PE.n
        lc[("h1q",)] = (PE, pef2)

        for tb in range(NTB3):
            @step("scalar",
                  waits=[(PE, pef2), (VEC, VEC.n), (D_OUT, D_OUT.n)],
                  inc=ACT)
            def _(E, link, fin, tb=tb):
                E.activation(t_ffn[:, 0:512], PS[tb * 2][:, :512], AF.Copy,
                             scale=col(t_consts, C_I1024))
                fin(E.activation(t_ffn[:, 512:1024], PS[tb * 2 + 1][:, :512],
                                 AF.Copy, scale=col(t_consts, C_I1024)))
            actf = ACT.n
            bankrd[tb * 2] = (ACT, ACT.n)
            bankrd[tb * 2 + 1] = (ACT, ACT.n)

            @step("vector", waits=[(ACT, actf)], inc=VEC, inck=2)
            def _(E, link, fin, tb=tb, r=r):
                link(E.tensor_add(t_ffn, t_ffn, t_b2row))
                fin(E.tensor_add(t_v3, t_ffn, t_ln1save[r % 2][:, tb, :]))
            vv2 = VEC.n

            @step("scalar", waits=[(VEC, vv2)], inc=ACT, inck=2)
            def _(E, link, fin):
                link(E.activation(t_ffn, t_v3, AF.Copy,
                                  accum_out=col(t_stats, S_S1)))
                fin(E.activation(t_ffn, t_v3, AF.Square,
                                 accum_out=col(t_stats, S_S2)))
            actst2 = ACT.n

            @step("vector", waits=[(ACT, actst2)], inc=VEC, inck=3)
            def _(E, link, fin):
                link(E.tensor_scalar_mul(col(t_stats, S_MU),
                                         col(t_stats, S_S1), 1.0 / DM))
                link(E.tensor_mul(col(t_stats, S_VAR), col(t_stats, S_MU),
                                  col(t_stats, S_MU)))
                fin(E.scalar_tensor_tensor(col(t_stats, S_VAR),
                                           col(t_stats, S_S2), 1.0 / DM,
                                           col(t_stats, S_VAR),
                                           ALU.mult, ALU.subtract))
            vvar2 = VEC.n

            @step("scalar", waits=[(VEC, vvar2)], inc=ACT, inck=2)
            def _(E, link, fin):
                link(E.activation(col(t_stats, S_RSTD), col(t_stats, S_VAR),
                                  AF.Ln, bias=col(t_consts, C_EPS)))
                fin(E.activation(col(t_stats, S_RSTD), col(t_stats, S_RSTD),
                                 AF.Exp, scale=col(t_consts, C_M05)))
            actrs2 = ACT.n

            @step("vector", waits=[(ACT, actrs2)], inc=VEC)
            def _(E, link, fin):
                fin(E.tensor_scalar(t_scr, t_v3, col(t_stats, S_MU),
                                    col(t_stats, S_RSTD), ALU.subtract,
                                    ALU.mult))
            vout = VEC.n

            @step("sync", waits=[(VEC, vout)], inc=D_OUT, inck=16)
            def _(E, link, fin, r=r, tb=tb):
                r0 = r * SUB + tb * 128
                fin(E.dma_start(out=d_out[r0:r0 + 128, :], in_=t_scr))
            yield

    def chain(*gens):
        for g in gens:
            yield from g

    def drive(gens, bg=None, bg_cap=0):
        gens = [g for g in gens if g is not None]
        n_bg = 0
        while gens or (bg is not None and n_bg < bg_cap):
            alive = []
            for g in gens:
                try:
                    next(g)
                    alive.append(g)
                except StopIteration:
                    pass
            gens = alive
            if bg is not None and n_bg < bg_cap:
                n_bg += 1
                try:
                    next(bg)
                except StopIteration:
                    bg = None
        return bg

    gwp = emit_WPREP()
    for tau in range(RND + 3):
        r2 = tau - 1
        r3 = tau - 2
        g2 = chain(*[emit_A2(bt, r2, j, (j * B + bt) % 2)
                     for (bt, j) in [(0, 0), (1, 0), (0, 1), (1, 1)]]) \
            if 1 <= tau <= RND else None
        g1 = chain(emit_A1a(0, tau), emit_A1b(0, tau),
                   emit_A1a(1, tau), emit_A1b(1, tau)) \
            if tau < RND else None
        parts3 = []
        if 0 <= r3 <= RND - 1:
            parts3.append(emit_A3a(r3))
        if 0 <= r3 - 1 <= RND - 1:
            parts3.append(emit_A3c(r3 - 1))
        if 0 <= r3 <= RND - 1:
            parts3.append(emit_A3b(r3))
        g3 = chain(*parts3) if parts3 else None
        gwp = drive([g2, g3, g1], bg=gwp, bg_cap=80)
        if 1 <= tau <= RND:
            r = tau - 1
            w = [yg_stored[(bt, r, j)] for bt in range(B) for j in range(NJ2)]

            @step("gpsimd", waits=w, inc=CC)
            def _(E, link, fin, r=r):
                from concourse.aot_env import allow_ici_all_to_all
                row0 = r * DI
                with allow_ici_all_to_all():
                    fin(E.collective_compute(
                        "AllToAll", ALU.bypass,
                        replica_groups=[list(range(NCORE))],
                        ins=[d_yg[row0:row0 + DI, :]],
                        outs=[d_ygx[row0:row0 + DI, :]]))
            cc2_done[r] = CC.n

    final_out = D_OUT.n

    @step("sync", waits=[(D_OUT, final_out)])
    def _(E, link, fin):
        pass

    with nc.Block() as blk:
        @blk.sync
        def _(eng):
            engobj.update(sync=nc.sync, gpsimd=nc.gpsimd, scalar=nc.scalar,
                          vector=nc.vector, tensor=nc.tensor)
            for fn in prog["sync"]:
                fn()

        @blk.gpsimd
        def _(eng):
            for fn in prog["gpsimd"]:
                fn()

        @blk.scalar
        def _(eng):
            for fn in prog["scalar"]:
                fn()

        @blk.vector
        def _(eng):
            for fn in prog["vector"]:
                fn()

        @blk.tensor
        def _(eng):
            for fn in prog["tensor"]:
                fn()

    es.close()
    return nc


def _mk_in_maps(inp):
    TTOT, DSL, TSL = _dims()
    DH = DSL // 128
    RND = L // C1
    SUB = B * C1 // NCORE
    x = np.ascontiguousarray(
        np.asarray(inp["L0_attn_x"], dtype=np.float32).reshape(TTOT, DM))
    win = np.asarray(inp["L0_attn_in_proj_w"], dtype=np.float32)
    inb = np.asarray(inp["L0_attn_in_proj_b"], dtype=np.float32)
    convw = np.asarray(inp["L0_attn_conv_w"], dtype=np.float32)[:, 0, :]
    convb = np.asarray(inp["L0_attn_conv_b"], dtype=np.float32)
    xpw = np.asarray(inp["L0_attn_x_proj_w"], dtype=np.float32)
    dtw = np.asarray(inp["L0_attn_dt_proj_w"], dtype=np.float32)
    dtb = np.asarray(inp["L0_attn_dt_proj_b"], dtype=np.float32)
    alog = np.asarray(inp["L0_attn_A_log"], dtype=np.float32)
    dpar = np.asarray(inp["L0_attn_D_param"], dtype=np.float32)
    ow = np.asarray(inp["L0_attn_out_proj_w"], dtype=np.float32)
    ob = np.asarray(inp["L0_attn_out_proj_b"], dtype=np.float32)
    w1 = np.asarray(inp["L0_ffn_w1"], dtype=np.float32)
    b1 = np.asarray(inp["L0_ffn_b1"], dtype=np.float32)
    w2 = np.asarray(inp["L0_ffn_w2"], dtype=np.float32)
    b2 = np.asarray(inp["L0_ffn_b2"], dtype=np.float32)

    import ml_dtypes
    idf = np.eye(128, dtype=np.float32)
    idb = np.eye(128, dtype=np.float32).astype(ml_dtypes.bfloat16)

    def dh_split(v):
        v = np.asarray(v)
        return np.ascontiguousarray(
            v.reshape(DH, 128, -1).transpose(1, 0, 2).reshape(128, -1))

    maps = []
    for c in range(NCORE):
        sl = slice(c * DSL, (c + 1) * DSL)
        rows = np.r_[c * DSL:(c + 1) * DSL, DI + c * DSL:DI + (c + 1) * DSL]
        winT = np.ascontiguousarray(win[rows, :].T)
        bt = c // (NCORE // B)
        ji = c % (NCORE // B)
        xm = np.concatenate(
            [x[bt * L + r * C1 + ji * SUB: bt * L + r * C1 + ji * SUB + SUB, :]
             for r in range(RND)], axis=0)
        m = dict(
            x=x,
            xmine=np.ascontiguousarray(xm),
            winT=winT,
            inb=np.concatenate([dh_split(inb[sl]),
                                dh_split(inb[DI + c * DSL:DI + (c + 1) * DSL])],
                               axis=1),
            convw=dh_split(convw[sl]),
            convb=dh_split(convb[sl]),
            xpwT=dh_split(np.ascontiguousarray(xpw[:, sl].T)),
            dtwT=np.ascontiguousarray(dtw[sl, :].T),
            dtb=dh_split(dtb[sl]),
            alog=dh_split(alog[sl, :]),
            dpar=dh_split(dpar[sl]),
            owT=np.ascontiguousarray(ow.T),
            ob=ob.reshape(1, DM),
            w1T=np.ascontiguousarray(
                w1.reshape(HID // 128, 128, DM // 128, 128)
                .transpose(0, 3, 2, 1).reshape(HID, DM)),
            b1=np.ascontiguousarray(b1.reshape(HID // 128, 128).T),
            w2T=np.ascontiguousarray(w2.T),
            b2=b2.reshape(1, DM),
            idf=idf, idb=idb,
        )
        maps.append(m)
    return maps


def kernel(**inp):
    TTOT, DSL, TSL = _dims()
    RND = L // C1
    SUB = B * C1 // NCORE
    in_maps = _mk_in_maps(inp)
    nc = build_program()
    res = run_bass_kernel_spmd(nc, in_maps, core_ids=list(range(NCORE)))
    out = np.empty((TTOT, DM), dtype=np.float32)
    for c in range(NCORE):
        o = np.asarray(res.results[c]["out"], dtype=np.float32)
        bt = c // (NCORE // B)
        ji = c % (NCORE // B)
        for r in range(RND):
            t0 = bt * L + r * C1 + ji * SUB
            out[t0:t0 + SUB, :] = o[r * SUB:(r + 1) * SUB, :]
    return out.reshape(B, L, DM)


# version 35
# speedup vs baseline: 1.0152x; 1.0152x over previous
import contextlib
import numpy as np
import concourse.bass as bass
import concourse.mybir as mybir
from concourse.bass_utils import run_bass_kernel_spmd

F32 = mybir.dt.float32
BF16 = mybir.dt.bfloat16
I32 = mybir.dt.int32
AF = mybir.ActivationFunctionType
ALU = mybir.AluOpType

B = 2
L = 8192
DM = 1024
DI = 2048
NST = 16
DTR = 64
HID = 4096
NCORE = 8

C1 = 1024
C2 = 512
NG = 2

LN_EPS = 1e-5
XPW = DTR + 2 * NST


def _dims():
    return B * L, DI // NCORE, B * L // NCORE


class Sem:
    def __init__(self, h):
        self.h = h
        self.n = 0

    def bump(self, k=1):
        self.n += k
        return self.n


class Arena:
    def __init__(self, handle, nbytes):
        self.h = handle
        self.nbytes = nbytes
        self.off = 0
        self.marks = []
        self.peak = 0

    def alloc(self, shape, dtype):
        esz = mybir.dt.size(dtype)
        n = int(np.prod(shape)) * esz
        off = (self.off + 15) & ~15
        assert off + n <= self.nbytes, f"arena overflow: {off}+{n} > {self.nbytes}"
        self.off = off + n
        self.peak = max(self.peak, self.off)
        ap = self.h.ap()[:, off:off + n].bitcast(dtype)
        if len(shape) > 1:
            names = " ".join(f"d{i}" for i in range(len(shape)))
            ap = ap.rearrange(f"p ({names}) -> p {names}",
                              **{f"d{i}": s for i, s in enumerate(shape)})
        return ap

    def push(self):
        self.marks.append(self.off)

    def pop(self):
        self.off = self.marks.pop()


def build_program():
    TTOT, DSL, TSL = _dims()
    DH = DSL // 128
    KDM = DM // 128
    KDI = DI // 128
    KH = HID // 128
    RND = L // C1
    NB1 = C1 // 512
    NTB1 = C1 // 128
    NJ2 = C1 // C2
    SUB = B * C1 // NCORE
    NTB3 = SUB // 128
    NGRP = NST // NG
    assert DSL % 128 == 0 and L % C1 == 0 and C1 % C2 == 0 and NJ2 == 2

    nc = bass.Bass("TRN2", target_bir_lowering=False, debug=False)

    d_x = nc.dram_tensor("x", [TTOT, DM], F32, kind="ExternalInput").ap()
    d_xmine = nc.dram_tensor("xmine", [TSL, DM], F32, kind="ExternalInput").ap()
    d_winT = nc.dram_tensor("winT", [DM, 2 * DSL], F32, kind="ExternalInput").ap()
    d_inb = nc.dram_tensor("inb", [128, 2 * DH], F32, kind="ExternalInput").ap()
    d_convw = nc.dram_tensor("convw", [128, DH * 4], F32, kind="ExternalInput").ap()
    d_convb = nc.dram_tensor("convb", [128, DH], F32, kind="ExternalInput").ap()
    d_xpwT = nc.dram_tensor("xpwT", [128, DH * XPW], F32, kind="ExternalInput").ap()
    d_dtwT = nc.dram_tensor("dtwT", [DTR, DSL], F32, kind="ExternalInput").ap()
    d_dtb = nc.dram_tensor("dtb", [128, DH], F32, kind="ExternalInput").ap()
    d_alog = nc.dram_tensor("alog", [128, DH * NST], F32, kind="ExternalInput").ap()
    d_dpar = nc.dram_tensor("dpar", [128, DH], F32, kind="ExternalInput").ap()
    d_owT = nc.dram_tensor("owT", [DI, DM], F32, kind="ExternalInput").ap()
    d_ob = nc.dram_tensor("ob", [1, DM], F32, kind="ExternalInput").ap()
    d_w1T = nc.dram_tensor("w1T", [HID, DM], F32, kind="ExternalInput").ap()
    d_b1 = nc.dram_tensor("b1", [128, KH], F32, kind="ExternalInput").ap()
    d_w2T = nc.dram_tensor("w2T", [HID, DM], F32, kind="ExternalInput").ap()
    d_b2 = nc.dram_tensor("b2", [1, DM], F32, kind="ExternalInput").ap()
    d_idf = nc.dram_tensor("idf", [128, 128], F32, kind="ExternalInput").ap()
    d_idb = nc.dram_tensor("idb", [128, 128], BF16, kind="ExternalInput").ap()

    d_out = nc.dram_tensor("out", [TSL, DM], F32, kind="ExternalOutput").ap()

    d_uscr = nc.dram_tensor("uscr", [DH, 128, TTOT], BF16, kind="Internal").ap()
    d_gscr = nc.dram_tensor("gscr", [DH, 128, TTOT], BF16, kind="Internal").ap()
    d_xp = nc.dram_tensor("xp", [RND * B * XPW, C1], F32,
                          kind="Internal").ap()
    d_xdbl = nc.dram_tensor("xdbl", [RND * B * XPW, C1], F32,
                            kind="Internal").ap()
    d_yg = nc.dram_tensor("yg", [RND * DI, SUB], BF16, kind="Internal").ap()
    d_ygx = nc.dram_tensor("ygx", [RND * DI, SUB], BF16, kind="Internal").ap()
    d_wq1 = nc.dram_tensor("wq1", [HID, DM], BF16, kind="Internal").ap()
    d_wq2 = nc.dram_tensor("wq2", [HID, DM], BF16, kind="Internal").ap()
    d_wqo = nc.dram_tensor("wqo", [DI, DM], BF16, kind="Internal").ap()
    d_bcb = nc.dram_tensor("bcb", [RND * B * NJ2 * 2 * NST, C2], BF16,
                           kind="Internal").ap()

    ENGS = ("sync", "gpsimd", "scalar", "vector", "tensor")
    prog = {e: [] for e in ENGS}
    engobj = {}

    es = contextlib.ExitStack()
    ARENA_BYTES = 212736
    arena_h = es.enter_context(
        nc.sbuf_tensor("arena", [128, ARENA_BYTES], mybir.dt.int8))
    A = Arena(arena_h, ARENA_BYTES)
    PS = [es.enter_context(nc.psum_tensor(f"ps{i}", [128, 512], F32))
          for i in range(8)]
    PSB = [p.ap().bitcast(BF16) for p in PS]

    def mksem(name):
        return Sem(es.enter_context(nc.semaphore(name)))
    CC = mksem("CC")
    PE = mksem("PE")
    ACT = mksem("ACT")
    VEC = mksem("VEC")
    D_P0L = mksem("D_P0L")
    D_P0S = mksem("D_P0S")
    D_WPL = [mksem("D_WPL0"), mksem("D_WPL1")]
    D_WPS = [mksem("D_WPS0"), mksem("D_WPS1")]
    D_X = [mksem("D_X0"), mksem("D_X1")]
    D_U = mksem("D_U")
    D_G = mksem("D_G")
    D_XPS = mksem("D_XPS")
    D_U2 = mksem("D_U2")
    D_XD = mksem("D_XD")
    D_BR = [mksem("D_BR0"), mksem("D_BR1")]
    D_CR = [mksem("D_CR0"), mksem("D_CR1")]
    D_BC = mksem("D_BC")
    D_YG = mksem("D_YG")
    D_YST = mksem("D_YST")
    D_XR = mksem("D_XR")
    D_W1 = [mksem(f"D_W1{i}") for i in range(4)]
    D_W2 = [mksem(f"D_W2{i}") for i in range(4)]
    D_WO = [mksem("D_WOA"), mksem("D_WOB")]
    D_OUT = mksem("D_OUT")
    ALLSEMS = (CC, PE, ACT, VEC, D_P0L, D_P0S,
               D_WPL[0], D_WPL[1], D_WPS[0], D_WPS[1],
               D_X[0], D_X[1], D_U, D_G,
               D_XPS,
               D_U2, D_XD, D_BR[0], D_BR[1], D_CR[0], D_CR[1], D_BC,
               D_YG, D_YST, D_XR, *D_W1, *D_W2,
               D_WO[0], D_WO[1], D_OUT)

    def step(eng, waits=(), inc=None, inck=1):
        wl = [(s.h, v) for (s, v) in waits if v > 0]
        base = inc.n if inc is not None else None
        if inc is not None:
            inc.bump(inck)

        def deco(fn):
            def run():
                E = engobj[eng]
                for h, v in wl:
                    E.wait_ge(h, v)
                if inc is None:
                    fn(E, None, None)
                    return
                st = {"c": base}

                def link(inst, k=1, wait=True):
                    st["c"] += k
                    assert st["c"] < base + inck, "link() overran inck"
                    inst.then_inc(inc.h, k)
                    if wait:
                        E.wait_ge(inc.h, st["c"])

                def fin(inst):
                    rem = (base + inck) - st["c"]
                    st["c"] = base + inck
                    inst.then_inc(inc.h, rem)
                fn(E, link, fin)
                assert st["c"] == base + inck, "fin() not called"
            prog[eng].append(run)
            return None
        return deco

    def snap():
        return [(s, s.n) for s in ALLSEMS]

    def join(snapshot):
        for e in ENGS:
            @step(e, waits=snapshot)
            def _(E, link, fin):
                pass

    def col(ap2d, i):
        return ap2d[:, i:i + 1]

    bankrd = {i: (VEC, 0) for i in range(8)}

    t_win = A.alloc([KDM, 2 * DSL], BF16)
    t_inb = A.alloc([2 * DH], F32)
    t_xpw = A.alloc([DH, XPW], BF16)
    t_dtw = A.alloc([DSL], BF16)
    t_dtb = A.alloc([DH], F32)
    t_convw = A.alloc([DH, 4], F32)
    t_convb = A.alloc([DH], F32)
    t_a = A.alloc([DH, NST], F32)
    t_dpar = A.alloc([DH], F32)
    t_obrow = A.alloc([DM], F32)
    t_b2row = A.alloc([DM], F32)
    t_b1q16 = A.alloc([KH], F32)
    t_idf = A.alloc([128], F32)
    t_idb = A.alloc([128], BF16)
    t_hst2 = A.alloc([B * DH, NST], F32)
    t_consts = A.alloc([8], F32)
    C_ONE, C_M1, C_M05, C_EPS, C_I1024 = 0, 1, 2, 3, 4

    A.push()
    stg = A.alloc([KDM * 2 * DSL], F32)
    stg2 = A.alloc([HID], F32)
    stgq = A.alloc([HID], F32)
    stgi = A.alloc([HID], I32)
    stgi2 = A.alloc([HID], I32)
    stgb = A.alloc([HID], BF16)

    @step("sync", inc=D_P0L, inck=16 * 10)
    def _(E, link, fin):
        for dst, src in [
            (stg.rearrange("p (a b) -> p a b", a=KDM),
             d_winT.rearrange("(a p) b -> p a b", p=128)),
            (t_inb, d_inb),
            (t_convw, d_convw.rearrange("p (a b) -> p a b", a=DH)),
            (t_convb, d_convb),
            (t_dtb, d_dtb),
            (t_dpar, d_dpar),
            (t_idf, d_idf),
            (t_idb, d_idb),
        ]:
            link(E.dma_start(out=dst, in_=src), 16, wait=False)
        link(E.dma_start(out=t_obrow,
                         in_=d_ob[None, 0, :].to_broadcast([128, DM])),
             16, wait=False)
        fin(E.dma_start(out=t_b2row,
                        in_=d_b2[None, 0, :].to_broadcast([128, DM])))
    ld0 = D_P0L.n

    @step("vector", waits=[(D_P0L, ld0)], inc=VEC)
    def _(E, link, fin):
        for i, v in [(C_ONE, 1.0), (C_M1, -1.0), (C_M05, -0.5),
                     (C_EPS, LN_EPS), (C_I1024, 1.0 / 1024.0)]:
            E.memset(t_consts[:, i:i + 1], v)
        E.memset(t_hst2.rearrange("p a b -> p (a b)"), 0.0)
        fin(E.tensor_copy(t_win.rearrange("p a b -> p (a b)"), stg))
    v_cast_win = VEC.n

    @step("sync", waits=[(VEC, v_cast_win)], inc=D_P0L, inck=16 * 3)
    def _(E, link, fin):
        link(E.dma_start(out=stg[:, :DH * XPW], in_=d_xpwT), 16, wait=False)
        link(E.dma_start(out=stg[0:DTR, DH * XPW:DH * XPW + DSL], in_=d_dtwT),
             16, wait=False)
        fin(E.dma_start(
            out=stg[:, DH * XPW + DSL:DH * XPW + DSL + DH * NST], in_=d_alog))
    ld1 = D_P0L.n

    @step("scalar", waits=[(D_P0L, ld1)], inc=ACT)
    def _(E, link, fin):
        fin(E.activation(t_a.rearrange("p a b -> p (a b)"),
                         stg[:, DH * XPW + DSL:DH * XPW + DSL + DH * NST],
                         AF.Exp))
    a_exp = ACT.n

    @step("vector", waits=[(ACT, a_exp)], inc=VEC)
    def _(E, link, fin):
        a_flat = t_a.rearrange("p a b -> p (a b)")
        E.tensor_scalar_mul(a_flat, a_flat, -1.0)
        E.tensor_copy(t_xpw.rearrange("p a b -> p (a b)"), stg[:, :DH * XPW])
        fin(E.tensor_copy(t_dtw[0:DTR, :],
                          stg[0:DTR, DH * XPW:DH * XPW + DSL]))

    def emit_rne(E, link, src_f32, n, scale):
        link(E.tensor_scalar(stgq[:, :n], src_f32, float(scale), 256.0,
                             ALU.mult, ALU.add))
        link(E.tensor_copy(stgi[:, :n], stgq[:, :n]))
        link(E.tensor_sub(stg2[:, :n], stgq[:, :n], stgi[:, :n]))
        link(E.tensor_single_scalar(stg2[:, :n], stg2[:, :n], 0.5, ALU.is_gt))
        link(E.tensor_add(stgi2[:, :n], stgi[:, :n], stg2[:, :n]))
        link(E.tensor_scalar(stgi2[:, :n], stgi2[:, :n], 256, -128,
                             ALU.subtract, ALU.max))

    @step("sync", waits=[(VEC, VEC.n)], inc=D_P0L, inck=16)
    def _(E, link, fin):
        fin(E.dma_start(out=stg[:, :KH], in_=d_b1))
    ldb1 = D_P0L.n

    @step("vector", waits=[(D_P0L, ldb1)], inc=VEC, inck=8)
    def _(E, link, fin):
        emit_rne(E, link, stg[:, :KH], KH, 64.0)
        link(E.tensor_single_scalar(stgi2[:, :KH], stgi2[:, :KH], 127, ALU.min))
        fin(E.tensor_scalar_mul(t_b1q16, stgi2[:, :KH], 16.0))

    @step("vector", waits=[(VEC, VEC.n)], inc=VEC, inck=8)
    def _(E, link, fin):
        emit_rne(E, link, t_b2row, DM, 64.0)
        link(E.tensor_single_scalar(stgi2[:, :DM], stgi2[:, :DM], 127, ALU.min))
        fin(E.tensor_scalar_mul(t_b2row, stgi2[:, :DM], 1.0 / 64.0))

    w1_src = d_w1T.rearrange("(a p) b -> p a b", p=128)
    wq1_dst = d_wq1.rearrange("(a p) b -> p a b", p=128)
    w2_src = d_w2T.rearrange("(a p) b -> p a b", p=128)
    wq2_dst = d_wq2.rearrange("(a p) b -> p a b", p=128)
    wo_src = d_owT.rearrange("(a p) b -> p a b", p=128)
    wqo_dst = d_wqo.rearrange("(a p) b -> p a b", p=128)
    A.pop()

    wprep_done = {}

    def emit_WPREP():
        units = []
        for kind, src, dst, ntile, quant in [
                ("wo", wo_src, wqo_dst, KDI, False),
                ("w1", w1_src, wq1_dst, KH, True),
                ("w2", w2_src, wq2_dst, KH, True)]:
            for t in range(ntile):
                for h in range(2):
                    units.append((kind, src, dst, t, h, quant))
        for u, (kind, src, dst, t, h, quant) in enumerate(units):
            sl = u % 2

            @step("sync", waits=[(D_WPS[sl], D_WPS[sl].n)],
                  inc=D_WPL[sl], inck=16)
            def _(E, link, fin, src=src, t=t, h=h, sl=sl):
                fin(E.dma_start(out=t_ws[sl],
                                in_=src[:, t, h * 512:(h + 1) * 512]))
            ldw = D_WPL[sl].n

            if quant:
                @step("vector", waits=[(D_WPL[sl], ldw),
                                       (D_WPS[sl], D_WPS[sl].n)],
                      inc=VEC, inck=7)
                def _(E, link, fin, sl=sl):
                    link(E.tensor_scalar(t_wq, t_ws[sl], 64.0, 256.0,
                                         ALU.mult, ALU.add))
                    link(E.tensor_copy(t_wi, t_wq))
                    link(E.tensor_sub(t_wq, t_wq, t_wi))
                    link(E.tensor_single_scalar(t_wq, t_wq, 0.5, ALU.is_gt))
                    link(E.tensor_add(t_wi2, t_wi, t_wq))
                    link(E.tensor_scalar(t_wi2, t_wi2, 256, -128,
                                         ALU.subtract, ALU.max))
                    fin(E.tensor_single_scalar(t_wb[sl], t_wi2, 127, ALU.min))
            else:
                @step("vector", waits=[(D_WPL[sl], ldw),
                                       (D_WPS[sl], D_WPS[sl].n)],
                      inc=VEC)
                def _(E, link, fin, sl=sl):
                    fin(E.tensor_copy(t_wb[sl], t_ws[sl]))
            vq = VEC.n

            @step("scalar", waits=[(VEC, vq)], inc=D_WPS[sl], inck=16)
            def _(E, link, fin, dst=dst, t=t, h=h, sl=sl):
                fin(E.dma_start(out=dst[:, t, h * 512:(h + 1) * 512],
                                in_=t_wb[sl]))
            wprep_done.setdefault((kind, t), []).append(
                (D_WPS[sl], D_WPS[sl].n))
            yield
    join(snap())

    t_xstg = [A.alloc([DM], F32) for _ in range(2)]
    t_xT = A.alloc([KDM, C1], BF16)
    t_xbraw = A.alloc([DH, C1 + 4], F32)
    t_tail = A.alloc([B, DH, 4], F32)
    t_acc_a = A.alloc([C1], F32)
    t_acc_b = A.alloc([C1], F32)
    t_u1b = A.alloc([DH, C1], BF16)
    t_g1 = A.alloc([DH, C1], BF16)
    t_xp1 = t_acc_a
    t_u2 = [A.alloc([DH, C2], BF16) for _ in range(2)]
    t_g2 = [A.alloc([DH, C2], BF16) for _ in range(2)]
    _xds = A.alloc([C2], F32)
    t_xd = [_xds, _xds]
    t_xdb = A.alloc([C2], BF16)
    t_z2 = A.alloc([DH, C2], F32)
    t_az = A.alloc([DH, C2], F32)
    t_delta = A.alloc([DH, C2], F32)
    t_du = A.alloc([DH, C2], BF16)
    t_brep = [A.alloc([NG, C2], BF16) for _ in range(2)]
    t_crep = [A.alloc([NG, C2], BF16) for _ in range(2)]
    t_dA = [A.alloc([NG, C2], F32) for _ in range(3)]
    t_dBu = A.alloc([NG, C2], BF16)
    t_h2 = A.alloc([NG, C2], BF16)
    t_y2 = t_z2
    t_yg2 = A.alloc([DH, C2], BF16)
    t_yT = A.alloc([KDI, SUB], BF16)
    t_wos = [A.alloc([DM], BF16) for _ in range(2)]
    t_v3 = A.alloc([DM], F32)
    t_xres = A.alloc([DM], F32)
    t_scr = t_v3
    t_stats = A.alloc([8], F32)
    S_S1, S_S2, S_MU, S_VAR, S_RSTD = 0, 1, 2, 3, 4
    t_ln1save = [A.alloc([NTB3, DM], F32) for _ in range(2)]
    t_qi = A.alloc([DM], I32)
    t_xqb = A.alloc([DM], BF16)
    t_x1T = A.alloc([KDM, SUB], BF16)
    t_w1s = [A.alloc([KDM, 128], BF16) for _ in range(4)]
    t_h1q = A.alloc([KH, SUB], BF16)
    t_w2s = [A.alloc([DM], BF16) for _ in range(4)]
    t_ffn = A.alloc([DM], F32)
    t_ws = [A.alloc([512], F32) for _ in range(2)]
    t_wq = A.alloc([512], F32)
    t_wi = t_qi[:, 0:512]
    t_wi2 = t_qi[:, 512:1024]
    t_wb = [A.alloc([512], BF16) for _ in range(2)]
    t_qf = t_ffn
    t_mi = t_qi[:, 0:SUB]
    t_mb = t_qi[:, SUB:2 * SUB]
    t_ms = t_mb
    assert A.peak <= A.nbytes, A.peak

    lc = {}

    def lcw(key):
        return lc.get(key, (VEC, 0))

    cc1_done = {}
    cc2_done = {}
    xp_stored = {}
    ug_stored = {}
    yg_stored = {}

    def emit_A1a(bt, r):
        tok0 = bt * L + r * C1
        ldx_of = {}

        def xload(tb):
            xp_par = tb % 2

            @step("scalar", waits=[lcw(("xstg", xp_par))], inc=D_X[xp_par],
                  inck=16)
            def _(E, link, fin, tok0=tok0, tb=tb, xp_par=xp_par):
                fin(E.dma_start(
                    out=t_xstg[xp_par],
                    in_=d_x[tok0 + tb * 128: tok0 + (tb + 1) * 128, :]))
            ldx_of[tb] = D_X[xp_par].n

        def xtrans(tb):
            xp_par = tb % 2
            ldx = ldx_of[tb]
            for g in range(2):
                bank = 4 + g
                @step("tensor", waits=[(D_X[xp_par], ldx), bankrd[bank]],
                      inc=PE)
                def _(E, link, fin, g=g, bank=bank, xp_par=xp_par):
                    for j in range(4):
                        kb = g * 4 + j
                        inst = E.transpose(
                            PS[bank][:, j * 128:(j + 1) * 128],
                            t_xstg[xp_par][:, kb * 128:(kb + 1) * 128],
                            t_idf)
                    fin(inst)
                pet = PE.n
                lc[("xstg", xp_par)] = (PE, PE.n)

                @step("scalar", waits=[(PE, pet)], inc=ACT)
                def _(E, link, fin, g=g, tb=tb, bank=bank):
                    for j in range(4):
                        kb = g * 4 + j
                        inst = E.copy(t_xT[:, kb, tb * 128:(tb + 1) * 128],
                                      PS[bank][:, j * 128:(j + 1) * 128])
                    fin(inst)
                bankrd[bank] = (ACT, ACT.n)

        for it in range(NTB1 + 2):
            if it >= 2:
                xtrans(it - 2)
            if it < NTB1:
                xload(it)
            if it >= 2:
                yield
        act_xt = ACT.n

        for s in range(NB1):
            for pb in range(4):
                bank = 6 + (pb % 2)
                @step("tensor", waits=[(ACT, act_xt), bankrd[bank]],
                      inc=PE)
                def _(E, link, fin, s=s, pb=pb, bank=bank):
                    for kb in range(KDM):
                        inst = E.matmul(
                            PS[bank][:, :512],
                            t_win[:, kb, pb * 128:(pb + 1) * 128],
                            t_xT[:, kb, s * 512:(s + 1) * 512],
                            start=(kb == 0), stop=(kb == KDM - 1))
                    fin(inst)
                pei = PE.n

                @step("scalar", waits=[(PE, pei), lcw(("a1evac", pb))],
                      inc=ACT)
                def _(E, link, fin, s=s, pb=pb, bank=bank):
                    if pb < 2:
                        fin(E.activation(
                            t_xbraw[:, pb, 3 + s * 512: 3 + (s + 1) * 512],
                            PS[bank][:, :512], AF.Identity,
                            bias=col(t_inb, pb)))
                    else:
                        fin(E.activation(
                            t_g1[:, pb - 2, s * 512:(s + 1) * 512],
                            PS[bank][:, :512], AF.Silu,
                            bias=col(t_inb, DH + pb - 2)))
                bankrd[bank] = (ACT, ACT.n)
                yield

    def emit_A1b(bt, r):
        tok0 = bt * L + r * C1
        act_inproj = ACT.n
        for dh in range(DH):
            @step("vector",
                  waits=[(ACT, act_inproj), (VEC, VEC.n),
                         (D_XPS, D_XPS.n), lcw(("accb",)), lcw(("acca",))],
                  inc=VEC, inck=7)
            def _(E, link, fin, dh=dh, bt=bt, r=r):
                src = t_xbraw[:, dh, :]
                if r == 0:
                    link(E.memset(src[:, 0:3], 0.0))
                else:
                    link(E.tensor_copy(src[:, 0:3], t_tail[:, bt, dh, 0:3]))
                link(E.tensor_scalar(t_acc_a, src[:, 0:C1],
                                     col(t_convw[:, dh, :], 0), None,
                                     ALU.mult))
                link(E.scalar_tensor_tensor(t_acc_b, src[:, 1:C1 + 1],
                                            col(t_convw[:, dh, :], 1),
                                            t_acc_a, ALU.mult, ALU.add))
                link(E.scalar_tensor_tensor(t_acc_a, src[:, 2:C1 + 2],
                                            col(t_convw[:, dh, :], 2),
                                            t_acc_b, ALU.mult, ALU.add))
                link(E.scalar_tensor_tensor(t_acc_b, src[:, 3:C1 + 3],
                                            col(t_convw[:, dh, :], 3),
                                            t_acc_a, ALU.mult, ALU.add))
                fin(E.tensor_copy(t_tail[:, bt, dh, 0:3], src[:, C1:C1 + 3]))
            vconv = VEC.n
            lc[("a1evac", dh)] = (VEC, vconv)

            @step("scalar", waits=[(VEC, vconv), lcw(("u1b",)),
                                   (D_U, D_U.n)],
                  inc=ACT)
            def _(E, link, fin, dh=dh):
                fin(E.activation(t_u1b[:, dh, :], t_acc_b, AF.Silu,
                                 bias=col(t_convb, dh)))
            lc[("accb",)] = (ACT, ACT.n)
            yield
        act_u = ACT.n

        for s in range(NB1):
            bank = 6
            @step("tensor", waits=[(ACT, act_u), bankrd[bank]], inc=PE)
            def _(E, link, fin, s=s, bank=bank):
                for dh in range(DH):
                    inst = E.matmul(PS[bank][0:XPW, :512],
                                    t_xpw[:, dh, :],
                                    t_u1b[:, dh, s * 512:(s + 1) * 512],
                                    start=(dh == 0), stop=(dh == DH - 1))
                fin(inst)
            pex = PE.n

            @step("scalar", waits=[(PE, pex), lcw(("xp1",))], inc=ACT)
            def _(E, link, fin, s=s, bank=bank):
                fin(E.copy(t_xp1[0:XPW, s * 512:(s + 1) * 512],
                           PS[bank][0:XPW, :512]))
            bankrd[bank] = (ACT, ACT.n)
            yield
        act_xp = ACT.n
        lc[("u1b",)] = (PE, PE.n)
        lc[("acca",)] = (ACT, act_xp)

        @step("sync", waits=[(ACT, act_xp)], inc=D_XPS, inck=16)
        def _(E, link, fin, r=r, bt=bt):
            rb0 = (r * B + bt) * XPW
            fin(E.dma_start(out=d_xp[rb0:rb0 + XPW, :],
                            in_=t_xp1[0:XPW, :]))
        lc[("xp1",)] = (D_XPS, D_XPS.n)

        @step("sync", inc=D_U, inck=32)
        def _(E, link, fin, tok0=tok0):
            link(E.dma_start(
                out=d_uscr[:, :, tok0:tok0 + C1].rearrange("a p c -> p a c"),
                in_=t_u1b), 16, wait=False)
            fin(E.dma_start(
                out=d_gscr[:, :, tok0:tok0 + C1].rearrange("a p c -> p a c"),
                in_=t_g1))
        lc[("a1evac", 2)] = (D_U, D_U.n)
        lc[("a1evac", 3)] = (D_U, D_U.n)
        ug_stored[(bt, r)] = (D_U, D_U.n)
        xp_stored[(bt, r)] = (D_XPS, D_XPS.n)

        @step("gpsimd", waits=[xp_stored[(bt, r)]], inc=CC)
        def _(E, link, fin, r=r, bt=bt):
            rb0 = (r * B + bt) * XPW
            fin(E.collective_compute(
                "AllReduce", ALU.add, replica_groups=[list(range(NCORE))],
                ins=[d_xp[rb0:rb0 + XPW, :]],
                outs=[d_xdbl[rb0:rb0 + XPW, :]]))
        cc1_done[(r, bt)] = CC.n
        yield


    def emit_A2(bt, r, j, P):
        tok0 = bt * L + r * C1 + j * C2
        cc1v = cc1_done[(r, bt)]

        @step("gpsimd",
              waits=[ug_stored[(bt, r)], lcw(("u2", P)),
                     lcw(("g2", P))],
              inc=D_U2, inck=32)
        def _(E, link, fin, tok0=tok0, P=P):
            link(E.dma_start(out=t_u2[P], in_=d_uscr[:, :, tok0:tok0 + C2]
                             .rearrange("a p c -> p a c")), 16, wait=False)
            fin(E.dma_start(out=t_g2[P], in_=d_gscr[:, :, tok0:tok0 + C2]
                            .rearrange("a p c -> p a c")))
        ldug = D_U2.n

        @step("gpsimd", waits=[(CC, cc1v), lcw(("xd", P))], inc=D_XD,
              inck=16)
        def _(E, link, fin, r=r, bt=bt, j=j, P=P):
            rb0 = (r * B + bt) * XPW
            fin(E.dma_start(out=t_xd[P][0:XPW, :],
                            in_=d_xdbl[rb0:rb0 + XPW,
                                       j * C2:(j + 1) * C2]))
        ldxd = D_XD.n

        @step("vector", waits=[(D_XD, ldxd), (VEC, VEC.n), lcw(("xdb",)),
                               lcw(("xdbst",))],
              inc=VEC)
        def _(E, link, fin, P=P):
            fin(E.tensor_copy(t_xdb[0:XPW, :], t_xd[P][0:XPW, :]))
        vxdb = VEC.n
        lc[("xd", P)] = (VEC, vxdb)

        bcrow0 = ((r * B + bt) * NJ2 + j) * 2 * NST

        @step("gpsimd", waits=[(VEC, vxdb)], inc=D_BC, inck=16)
        def _(E, link, fin, bcrow0=bcrow0):
            fin(E.dma_start(out=d_bcb[bcrow0:bcrow0 + 2 * NST, :],
                            in_=t_xdb[DTR:XPW, :]))
        lc[("xdbst",)] = (D_BC, D_BC.n)
        bcv = D_BC.n
        yield

        @step("tensor", waits=[(VEC, vxdb), bankrd[4], bankrd[5]], inc=PE)
        def _(E, link, fin):
            for pb in range(DH):
                inst = E.matmul(PS[4 + pb][:, :C2],
                                t_dtw[0:DTR, pb * 128:(pb + 1) * 128],
                                t_xdb[0:DTR, :],
                                start=True, stop=True)
            fin(inst)
        pedt = PE.n
        lc[("xdb",)] = (PE, pedt)

        @step("scalar", waits=[(PE, pedt), lcw(("z2",)), lcw(("az",))],
              inc=ACT, inck=4)
        def _(E, link, fin):
            for pb in range(DH):
                link(E.activation(t_z2[:, pb, :], PS[4 + pb][:, :C2],
                                  AF.Identity, bias=col(t_dtb, pb)))
            for pb in range(DH):
                inst = E.activation(t_az[:, pb, :], PS[4 + pb][:, :C2],
                                    AF.Abs, bias=col(t_dtb, pb))
                if pb < DH - 1:
                    link(inst)
            fin(inst)
        bankrd[4] = (ACT, ACT.n)
        bankrd[5] = (ACT, ACT.n)

        @step("scalar", waits=[(ACT, ACT.n)], inc=ACT, inck=2)
        def _(E, link, fin):
            azf = t_az.rearrange("p a b -> p (a b)")
            link(E.activation(azf, azf, AF.Exp, scale=-1.0))
            fin(E.activation(azf, azf, AF.Ln, bias=col(t_consts, C_ONE)))
        actsp = ACT.n
        yield

        for dh in range(DH):
            @step("vector",
                  waits=[(ACT, actsp), (D_U2, ldug), lcw(("dadu", dh))],
                  inc=VEC, inck=2)
            def _(E, link, fin, dh=dh):
                link(E.scalar_tensor_tensor(t_delta[:, dh, :],
                                            t_z2[:, dh, :], 0.0,
                                            t_az[:, dh, :],
                                            ALU.max, ALU.add))
                fin(E.tensor_mul(t_du[:, dh, :], t_delta[:, dh, :],
                                 t_u2[P][:, dh, :]))
            lc[("az",)] = (VEC, VEC.n)
            yield
        vdu = VEC.n

        groups = [(g, dh) for g in range(NGRP) for dh in range(DH)]
        NGT = len(groups)
        ld_br = {}
        ld_cr = {}

        def emit_load(i):
            g = groups[i][0]
            gp = g % 2

            @step("scalar", waits=[(D_BC, bcv), lcw(("brep", gp))],
                  inc=D_BR[gp], inck=16)
            def _(E, link, fin, g=g, gp=gp, bcrow0=bcrow0):
                r0 = bcrow0 + g * NG
                fin(E.dma_start(
                    out=t_brep[gp],
                    in_=d_bcb[None, r0:r0 + NG, :]
                    .to_broadcast([128, NG, C2])))
            ld_br[g] = D_BR[gp].n

            @step("gpsimd", waits=[(D_BC, bcv), lcw(("crep", gp))],
                  inc=D_CR[gp], inck=16)
            def _(E, link, fin, g=g, gp=gp, bcrow0=bcrow0):
                r0 = bcrow0 + NST + g * NG
                fin(E.dma_start(
                    out=t_crep[gp],
                    in_=d_bcb[None, r0:r0 + NG, :]
                    .to_broadcast([128, NG, C2])))
            ld_cr[g] = D_CR[gp].n

        def emit_exp(i):
            g, dh = groups[i]
            ip = i % 3

            @step("scalar", waits=[(VEC, vdu), lcw(("dA", ip))], inc=ACT)
            def _(E, link, fin, dh=dh, g=g, ip=ip):
                for n in range(NG):
                    inst = E.activation(t_dA[ip][:, n, :],
                                        t_delta[:, dh, :], AF.Exp,
                                        scale=t_a[:, dh, g * NG + n:
                                                  g * NG + n + 1])
                fin(inst)
            return ACT.n

        actda_of = {}

        def emit_scan(i):
            g, dh = groups[i]
            gp = g % 2
            ip = i % 3
            actda = actda_of[i]

            @step("vector",
                  waits=[(ACT, actda), (D_BR[gp], ld_br[g]),
                         (D_CR[gp], ld_cr[g]), (VEC, VEC.n)],
                  inc=VEC, inck=6)
            def _(E, link, fin, dh=dh, g=g, gp=gp, ip=ip, bt=bt):
                link(E.tensor_mul(
                    t_dBu,
                    t_du[:, dh, :][:, None, :].to_broadcast([128, NG, C2]),
                    t_brep[gp]))
                for n in range(NG):
                    inst = E.tensor_tensor_scan(
                        t_h2[:, n, :], t_dA[ip][:, n, :], t_dBu[:, n, :],
                        t_hst2[:, bt * DH + dh, g * NG + n:g * NG + n + 1],
                        ALU.mult, ALU.add)
                link(inst)
                link(E.tensor_copy(
                    t_hst2[:, bt * DH + dh, g * NG:(g + 1) * NG],
                    t_h2[:, :, C2 - 1:C2].rearrange("p a b -> p (a b)")))
                link(E.tensor_mul(t_dBu, t_h2, t_crep[gp]))
                if g == 0:
                    fin(E.tensor_add(t_y2[:, dh, :], t_dBu[:, 0, :],
                                     t_dBu[:, 1, :]))
                else:
                    link(E.tensor_add(t_y2[:, dh, :], t_y2[:, dh, :],
                                      t_dBu[:, 0, :]))
                    fin(E.tensor_add(t_y2[:, dh, :], t_y2[:, dh, :],
                                     t_dBu[:, 1, :]))
            lc[("dA", ip)] = (VEC, VEC.n)
            if dh == DH - 1:
                lc[("brep", gp)] = (VEC, VEC.n)
                lc[("crep", gp)] = (VEC, VEC.n)
            lc[("dadu", dh)] = (VEC, VEC.n)
            if g == NGRP - 1:
                @step("vector", waits=[(VEC, VEC.n), lcw(("yg2",))],
                      inc=VEC, inck=2)
                def _(E, link, fin, dh=dh, P=P):
                    link(E.scalar_tensor_tensor(t_y2[:, dh, :],
                                                t_u2[P][:, dh, :],
                                                col(t_dpar, dh),
                                                t_y2[:, dh, :],
                                                ALU.mult, ALU.add))
                    fin(E.tensor_mul(t_yg2[:, dh, :], t_y2[:, dh, :],
                                     t_g2[P][:, dh, :]))
                lc[("z2",)] = (VEC, VEC.n)

        for i in range(NGT + 2):
            if i >= 2:
                emit_scan(i - 2)
            if i < NGT:
                actda_of[i] = emit_exp(i)
            if i < NGT and groups[i][1] == 0:
                emit_load(i)
            if i >= 2:
                yield
        vyg = VEC.n
        lc[("u2", P)] = (VEC, vyg)
        lc[("g2", P)] = (VEC, vyg)

        NDD = C2 // SUB

        @step("sync", waits=[(VEC, vyg)], inc=D_YG, inck=16 * NDD)
        def _(E, link, fin, r=r, bt=bt, j=j):
            dest0 = (bt * C1 + j * C2) // SUB
            for dd in range(NDD):
                row0 = r * DI + (dest0 + dd) * DSL
                inst = E.dma_start(
                    out=d_yg[row0:row0 + DSL, :]
                    .rearrange("(a p) c -> p a c", p=128),
                    in_=t_yg2[:, :, dd * SUB:(dd + 1) * SUB])
                if dd < NDD - 1:
                    link(inst, 16, wait=False)
            fin(inst)
        lc[("yg2",)] = (D_YG, D_YG.n)
        yg_stored[(bt, r, j)] = (D_YG, D_YG.n)

    def emit_A3a(r):
        cc2v = cc2_done[r]
        @step("sync", waits=[(CC, cc2v), lcw(("yT",))],
              inc=D_YST, inck=16)
        def _(E, link, fin, r=r):
            fin(E.dma_start(
                out=t_yT,
                in_=d_ygx[r * DI:(r + 1) * DI, :]
                .rearrange("(a p) c -> p a c", p=128)))
        ldyT = D_YST.n
        yield

        for kb in range(KDI):
            wpar = kb % 2
            prev = lc.get(("wo", wpar), (VEC, 0))
            @step("sync", waits=[prev] + wprep_done[("wo", kb)],
                  inc=D_WO[wpar], inck=16)
            def _(E, link, fin, kb=kb, wpar=wpar):
                fin(E.dma_start(out=t_wos[wpar],
                                in_=wqo_dst[:, kb, :]))
            ldwo = D_WO[wpar].n

            waits = [(D_YST, ldyT), (D_WO[wpar], ldwo)]
            if kb == 0:
                waits += [bankrd[i] for i in range(4)]
            @step("tensor", waits=waits, inc=PE)
            def _(E, link, fin, kb=kb, wpar=wpar):
                for tb in range(NTB3):
                    for half in range(2):
                        inst = E.matmul(
                            PS[tb * 2 + half][:, :512],
                            t_yT[:, kb, tb * 128:(tb + 1) * 128],
                            t_wos[wpar][:, half * 512:(half + 1) * 512],
                            start=(kb == 0), stop=(kb == KDI - 1))
                fin(inst)
            lc[("wo", wpar)] = (PE, PE.n)
            yield
        peop = PE.n
        lc[("yT",)] = (PE, peop)

        for tb in range(NTB3):
            @step("sync", waits=[lcw(("xres",))], inc=D_XR, inck=16)
            def _(E, link, fin, r=r, tb=tb):
                r0 = r * SUB + tb * 128
                fin(E.dma_start(out=t_xres, in_=d_xmine[r0:r0 + 128, :]))
            ldr = D_XR.n

            @step("vector", waits=[(PE, peop), (D_XR, ldr), (VEC, VEC.n),
                                   (D_OUT, D_OUT.n), lcw(("v3",))],
                  inc=VEC, inck=3)
            def _(E, link, fin, tb=tb):
                link(E.tensor_add(t_v3[:, 0:512], PS[tb * 2][:, :512],
                                  t_obrow[:, 0:512]))
                link(E.tensor_add(t_v3[:, 512:1024], PS[tb * 2 + 1][:, :512],
                                  t_obrow[:, 512:1024]))
                fin(E.tensor_add(t_v3, t_v3, t_xres))
            vv = VEC.n
            bankrd[tb * 2] = (VEC, vv)
            bankrd[tb * 2 + 1] = (VEC, vv)
            lc[("xres",)] = (VEC, vv)

            @step("scalar", waits=[(VEC, vv), (D_OUT, D_OUT.n)],
                  inc=ACT, inck=2)
            def _(E, link, fin):
                link(E.activation(t_ffn, t_v3, AF.Copy,
                                  accum_out=col(t_stats, S_S1)))
                fin(E.activation(t_ffn, t_v3, AF.Square,
                                 accum_out=col(t_stats, S_S2)))
            actst = ACT.n

            @step("vector", waits=[(ACT, actst)], inc=VEC, inck=3)
            def _(E, link, fin):
                link(E.tensor_scalar_mul(col(t_stats, S_MU),
                                         col(t_stats, S_S1), 1.0 / DM))
                link(E.tensor_mul(col(t_stats, S_VAR), col(t_stats, S_MU),
                                  col(t_stats, S_MU)))
                fin(E.scalar_tensor_tensor(col(t_stats, S_VAR),
                                           col(t_stats, S_S2), 1.0 / DM,
                                           col(t_stats, S_VAR),
                                           ALU.mult, ALU.subtract))
            vvar = VEC.n

            @step("scalar", waits=[(VEC, vvar)], inc=ACT, inck=2)
            def _(E, link, fin):
                link(E.activation(col(t_stats, S_RSTD), col(t_stats, S_VAR),
                                  AF.Ln, bias=col(t_consts, C_EPS)))
                fin(E.activation(col(t_stats, S_RSTD), col(t_stats, S_RSTD),
                                 AF.Exp, scale=col(t_consts, C_M05)))
            actrs = ACT.n

            @step("vector", waits=[(ACT, actrs)], inc=VEC, inck=9)
            def _(E, link, fin, tb=tb):
                link(E.tensor_scalar(t_scr, t_v3, col(t_stats, S_MU),
                                     col(t_stats, S_RSTD), ALU.subtract,
                                     ALU.mult))
                link(E.tensor_copy(t_ln1save[r % 2][:, tb, :], t_scr))
                link(E.tensor_scalar(t_qf, t_scr, 16.0, 256.0, ALU.mult,
                                     ALU.add))
                link(E.tensor_copy(t_qi, t_qf))
                link(E.tensor_sub(t_qf, t_qf, t_qi))
                link(E.tensor_single_scalar(t_qf, t_qf, 0.5, ALU.is_gt))
                link(E.tensor_add(t_qi, t_qi, t_qf))
                link(E.tensor_scalar(t_qi, t_qi, 256, -128, ALU.subtract,
                                     ALU.max))
                fin(E.tensor_single_scalar(t_xqb, t_qi, 127, ALU.min))
            vxq = VEC.n
            lc[("v3",)] = (VEC, vxq)
            lc[("qi",)] = (VEC, vxq)
            yield

            for g in range(2):
                bank = 4 + g
                @step("tensor", waits=[(VEC, vxq), bankrd[bank]], inc=PE)
                def _(E, link, fin, g=g, bank=bank):
                    for j in range(4):
                        kb = g * 4 + j
                        inst = E.transpose(
                            PSB[bank][:, j * 128:(j + 1) * 128],
                            t_xqb[:, kb * 128:(kb + 1) * 128], t_idb)
                    fin(inst)
                pexq = PE.n

                @step("scalar", waits=[(PE, pexq)], inc=ACT)
                def _(E, link, fin, g=g, tb=tb, bank=bank):
                    for j in range(4):
                        kb = g * 4 + j
                        inst = E.copy(t_x1T[:, kb, tb * 128:(tb + 1) * 128],
                                      PSB[bank][:, j * 128:(j + 1) * 128])
                    fin(inst)
                bankrd[bank] = (ACT, ACT.n)
                lc[("x1T",)] = (ACT, ACT.n)
                yield

    def emit_A3b(r):
        act_x1t = lc.get(("x1T",), (ACT, 0))[1]
        for hb in range(KH):
            wpar = hb % 4
            bank = 6 + hb % 2
            prev = lc.get(("w1", wpar), (VEC, 0))
            @step("sync", waits=[prev] + wprep_done[("w1", hb)],
                  inc=D_W1[wpar], inck=16)
            def _(E, link, fin, hb=hb, wpar=wpar):
                src = d_wq1[hb * 128:(hb + 1) * 128, :]
                fin(E.dma_start(
                    out=t_w1s[wpar].rearrange("p a b -> p (a b)"), in_=src))
            ldw1 = D_W1[wpar].n

            @step("tensor",
                  waits=[(ACT, act_x1t), (D_W1[wpar], ldw1), bankrd[bank]],
                  inc=PE)
            def _(E, link, fin, hb=hb, wpar=wpar, bank=bank):
                for kb in range(KDM):
                    inst = E.matmul(PS[bank][:, :SUB],
                                    t_w1s[wpar][:, kb, :],
                                    t_x1T[:, kb, :],
                                    start=(kb == 0), stop=(kb == KDM - 1))
                fin(inst)
            lc[("w1", wpar)] = (PE, PE.n)
            peh = PE.n

            @step("scalar", waits=[(PE, peh), lcw(("qi",)), lcw(("mi",))],
                  inc=ACT)
            def _(E, link, fin, hb=hb, bank=bank):
                fin(E.activation(t_mi, PS[bank][:, :SUB], AF.Identity,
                                 bias=col(t_b1q16, hb)))
            actmi = ACT.n
            bankrd[bank] = (ACT, ACT.n)

            @step("vector", waits=[(ACT, actmi), (VEC, VEC.n),
                                   lcw(("h1q",))],
                  inc=VEC, inck=5)
            def _(E, link, fin, hb=hb, bank=bank):
                link(E.tensor_single_scalar(t_mb, t_mi, 6,
                                            ALU.arith_shift_right))
                link(E.tensor_single_scalar(t_mb, t_mb, 1, ALU.bitwise_and))
                link(E.scalar_tensor_tensor(t_ms, t_mi, 31, t_mb, ALU.add,
                                            ALU.add))
                link(E.tensor_single_scalar(t_ms, t_ms, 6,
                                            ALU.arith_shift_right))
                fin(E.tensor_scalar(t_h1q[:, hb, :], t_ms, 0, 127,
                                    ALU.max, ALU.min))
            lc[("mi",)] = (VEC, VEC.n)
            yield
        v_h1 = VEC.n

    def emit_A3c(r):
        v_h1 = VEC.n
        for kb in range(KH):
            wpar = kb % 4
            prev = lc.get(("w2", wpar), (VEC, 0))
            @step("sync", waits=[prev] + wprep_done[("w2", kb)],
                  inc=D_W2[wpar], inck=16)
            def _(E, link, fin, kb=kb, wpar=wpar):
                src = d_wq2.rearrange("(a p) b -> p a b", p=128)[:, kb, :]
                fin(E.dma_start(out=t_w2s[wpar], in_=src))
            ldw2 = D_W2[wpar].n

            waits = [(VEC, v_h1), (D_W2[wpar], ldw2)]
            if kb == 0:
                waits += [bankrd[i] for i in range(4)]
            @step("tensor", waits=waits, inc=PE)
            def _(E, link, fin, kb=kb, wpar=wpar):
                for tb in range(NTB3):
                    for half in range(2):
                        inst = E.matmul(
                            PS[tb * 2 + half][:, :512],
                            t_h1q[:, kb, tb * 128:(tb + 1) * 128],
                            t_w2s[wpar][:, half * 512:(half + 1) * 512],
                            start=(kb == 0), stop=(kb == KH - 1))
                fin(inst)
            lc[("w2", wpar)] = (PE, PE.n)
            yield
        pef2 = PE.n
        lc[("h1q",)] = (PE, pef2)

        for tb in range(NTB3):
            @step("scalar",
                  waits=[(PE, pef2), (VEC, VEC.n), (D_OUT, D_OUT.n)],
                  inc=ACT)
            def _(E, link, fin, tb=tb):
                E.activation(t_ffn[:, 0:512], PS[tb * 2][:, :512], AF.Copy,
                             scale=col(t_consts, C_I1024))
                fin(E.activation(t_ffn[:, 512:1024], PS[tb * 2 + 1][:, :512],
                                 AF.Copy, scale=col(t_consts, C_I1024)))
            actf = ACT.n
            bankrd[tb * 2] = (ACT, ACT.n)
            bankrd[tb * 2 + 1] = (ACT, ACT.n)

            @step("vector", waits=[(ACT, actf)], inc=VEC, inck=2)
            def _(E, link, fin, tb=tb, r=r):
                link(E.tensor_add(t_ffn, t_ffn, t_b2row))
                fin(E.tensor_add(t_v3, t_ffn, t_ln1save[r % 2][:, tb, :]))
            vv2 = VEC.n

            @step("scalar", waits=[(VEC, vv2)], inc=ACT, inck=2)
            def _(E, link, fin):
                link(E.activation(t_ffn, t_v3, AF.Copy,
                                  accum_out=col(t_stats, S_S1)))
                fin(E.activation(t_ffn, t_v3, AF.Square,
                                 accum_out=col(t_stats, S_S2)))
            actst2 = ACT.n

            @step("vector", waits=[(ACT, actst2)], inc=VEC, inck=3)
            def _(E, link, fin):
                link(E.tensor_scalar_mul(col(t_stats, S_MU),
                                         col(t_stats, S_S1), 1.0 / DM))
                link(E.tensor_mul(col(t_stats, S_VAR), col(t_stats, S_MU),
                                  col(t_stats, S_MU)))
                fin(E.scalar_tensor_tensor(col(t_stats, S_VAR),
                                           col(t_stats, S_S2), 1.0 / DM,
                                           col(t_stats, S_VAR),
                                           ALU.mult, ALU.subtract))
            vvar2 = VEC.n

            @step("scalar", waits=[(VEC, vvar2)], inc=ACT, inck=2)
            def _(E, link, fin):
                link(E.activation(col(t_stats, S_RSTD), col(t_stats, S_VAR),
                                  AF.Ln, bias=col(t_consts, C_EPS)))
                fin(E.activation(col(t_stats, S_RSTD), col(t_stats, S_RSTD),
                                 AF.Exp, scale=col(t_consts, C_M05)))
            actrs2 = ACT.n

            @step("vector", waits=[(ACT, actrs2)], inc=VEC)
            def _(E, link, fin):
                fin(E.tensor_scalar(t_scr, t_v3, col(t_stats, S_MU),
                                    col(t_stats, S_RSTD), ALU.subtract,
                                    ALU.mult))
            vout = VEC.n

            @step("sync", waits=[(VEC, vout)], inc=D_OUT, inck=16)
            def _(E, link, fin, r=r, tb=tb):
                r0 = r * SUB + tb * 128
                fin(E.dma_start(out=d_out[r0:r0 + 128, :], in_=t_scr))
            yield

    def chain(*gens):
        for g in gens:
            yield from g

    def drive(gens, bg=None, bg_cap=0):
        gens = [g for g in gens if g is not None]
        n_bg = 0
        while gens or (bg is not None and n_bg < bg_cap):
            alive = []
            for g in gens:
                try:
                    next(g)
                    alive.append(g)
                except StopIteration:
                    pass
            gens = alive
            if bg is not None and n_bg < bg_cap:
                n_bg += 1
                try:
                    next(bg)
                except StopIteration:
                    bg = None
        return bg

    gwp = emit_WPREP()
    for tau in range(RND + 3):
        r2 = tau - 1
        r3 = tau - 2
        g2 = chain(*[emit_A2(bt, r2, j, (j * B + bt) % 2)
                     for (bt, j) in [(0, 0), (1, 0), (0, 1), (1, 1)]]) \
            if 1 <= tau <= RND else None
        g1 = chain(emit_A1a(0, tau), emit_A1b(0, tau),
                   emit_A1a(1, tau), emit_A1b(1, tau)) \
            if tau < RND else None
        parts3 = []
        if 0 <= r3 <= RND - 1:
            parts3.append(emit_A3a(r3))
        if 0 <= r3 - 1 <= RND - 1:
            parts3.append(emit_A3c(r3 - 1))
        if 0 <= r3 <= RND - 1:
            parts3.append(emit_A3b(r3))
        g3 = chain(*parts3) if parts3 else None
        gwp = drive([g2, g1, g3], bg=gwp, bg_cap=80)
        if 1 <= tau <= RND:
            r = tau - 1
            w = [yg_stored[(bt, r, j)] for bt in range(B) for j in range(NJ2)]

            @step("gpsimd", waits=w, inc=CC)
            def _(E, link, fin, r=r):
                from concourse.aot_env import allow_ici_all_to_all
                row0 = r * DI
                with allow_ici_all_to_all():
                    fin(E.collective_compute(
                        "AllToAll", ALU.bypass,
                        replica_groups=[list(range(NCORE))],
                        ins=[d_yg[row0:row0 + DI, :]],
                        outs=[d_ygx[row0:row0 + DI, :]]))
            cc2_done[r] = CC.n

    final_out = D_OUT.n

    @step("sync", waits=[(D_OUT, final_out)])
    def _(E, link, fin):
        pass

    with nc.Block() as blk:
        @blk.sync
        def _(eng):
            engobj.update(sync=nc.sync, gpsimd=nc.gpsimd, scalar=nc.scalar,
                          vector=nc.vector, tensor=nc.tensor)
            for fn in prog["sync"]:
                fn()

        @blk.gpsimd
        def _(eng):
            for fn in prog["gpsimd"]:
                fn()

        @blk.scalar
        def _(eng):
            for fn in prog["scalar"]:
                fn()

        @blk.vector
        def _(eng):
            for fn in prog["vector"]:
                fn()

        @blk.tensor
        def _(eng):
            for fn in prog["tensor"]:
                fn()

    es.close()
    return nc


def _mk_in_maps(inp):
    TTOT, DSL, TSL = _dims()
    DH = DSL // 128
    RND = L // C1
    SUB = B * C1 // NCORE
    x = np.ascontiguousarray(
        np.asarray(inp["L0_attn_x"], dtype=np.float32).reshape(TTOT, DM))
    win = np.asarray(inp["L0_attn_in_proj_w"], dtype=np.float32)
    inb = np.asarray(inp["L0_attn_in_proj_b"], dtype=np.float32)
    convw = np.asarray(inp["L0_attn_conv_w"], dtype=np.float32)[:, 0, :]
    convb = np.asarray(inp["L0_attn_conv_b"], dtype=np.float32)
    xpw = np.asarray(inp["L0_attn_x_proj_w"], dtype=np.float32)
    dtw = np.asarray(inp["L0_attn_dt_proj_w"], dtype=np.float32)
    dtb = np.asarray(inp["L0_attn_dt_proj_b"], dtype=np.float32)
    alog = np.asarray(inp["L0_attn_A_log"], dtype=np.float32)
    dpar = np.asarray(inp["L0_attn_D_param"], dtype=np.float32)
    ow = np.asarray(inp["L0_attn_out_proj_w"], dtype=np.float32)
    ob = np.asarray(inp["L0_attn_out_proj_b"], dtype=np.float32)
    w1 = np.asarray(inp["L0_ffn_w1"], dtype=np.float32)
    b1 = np.asarray(inp["L0_ffn_b1"], dtype=np.float32)
    w2 = np.asarray(inp["L0_ffn_w2"], dtype=np.float32)
    b2 = np.asarray(inp["L0_ffn_b2"], dtype=np.float32)

    import ml_dtypes
    idf = np.eye(128, dtype=np.float32)
    idb = np.eye(128, dtype=np.float32).astype(ml_dtypes.bfloat16)

    def dh_split(v):
        v = np.asarray(v)
        return np.ascontiguousarray(
            v.reshape(DH, 128, -1).transpose(1, 0, 2).reshape(128, -1))

    maps = []
    for c in range(NCORE):
        sl = slice(c * DSL, (c + 1) * DSL)
        rows = np.r_[c * DSL:(c + 1) * DSL, DI + c * DSL:DI + (c + 1) * DSL]
        winT = np.ascontiguousarray(win[rows, :].T)
        bt = c // (NCORE // B)
        ji = c % (NCORE // B)
        xm = np.concatenate(
            [x[bt * L + r * C1 + ji * SUB: bt * L + r * C1 + ji * SUB + SUB, :]
             for r in range(RND)], axis=0)
        m = dict(
            x=x,
            xmine=np.ascontiguousarray(xm),
            winT=winT,
            inb=np.concatenate([dh_split(inb[sl]),
                                dh_split(inb[DI + c * DSL:DI + (c + 1) * DSL])],
                               axis=1),
            convw=dh_split(convw[sl]),
            convb=dh_split(convb[sl]),
            xpwT=dh_split(np.ascontiguousarray(xpw[:, sl].T)),
            dtwT=np.ascontiguousarray(dtw[sl, :].T),
            dtb=dh_split(dtb[sl]),
            alog=dh_split(alog[sl, :]),
            dpar=dh_split(dpar[sl]),
            owT=np.ascontiguousarray(ow.T),
            ob=ob.reshape(1, DM),
            w1T=np.ascontiguousarray(
                w1.reshape(HID // 128, 128, DM // 128, 128)
                .transpose(0, 3, 2, 1).reshape(HID, DM)),
            b1=np.ascontiguousarray(b1.reshape(HID // 128, 128).T),
            w2T=np.ascontiguousarray(w2.T),
            b2=b2.reshape(1, DM),
            idf=idf, idb=idb,
        )
        maps.append(m)
    return maps


def kernel(**inp):
    TTOT, DSL, TSL = _dims()
    RND = L // C1
    SUB = B * C1 // NCORE
    in_maps = _mk_in_maps(inp)
    nc = build_program()
    res = run_bass_kernel_spmd(nc, in_maps, core_ids=list(range(NCORE)))
    out = np.empty((TTOT, DM), dtype=np.float32)
    for c in range(NCORE):
        o = np.asarray(res.results[c]["out"], dtype=np.float32)
        bt = c // (NCORE // B)
        ji = c % (NCORE // B)
        for r in range(RND):
            t0 = bt * L + r * C1 + ji * SUB
            out[t0:t0 + SUB, :] = o[r * SUB:(r + 1) * SUB, :]
    return out.reshape(B, L, DM)


# version 36 (final)
# speedup vs baseline: 1.0172x; 1.0020x over previous
import contextlib
import numpy as np
import concourse.bass as bass
import concourse.mybir as mybir
from concourse.bass_utils import run_bass_kernel_spmd

F32 = mybir.dt.float32
BF16 = mybir.dt.bfloat16
I32 = mybir.dt.int32
AF = mybir.ActivationFunctionType
ALU = mybir.AluOpType

B = 2
L = 8192
DM = 1024
DI = 2048
NST = 16
DTR = 64
HID = 4096
NCORE = 8

C1 = 1024
C2 = 512
NG = 2

LN_EPS = 1e-5
XPW = DTR + 2 * NST


def _dims():
    return B * L, DI // NCORE, B * L // NCORE


class Sem:
    def __init__(self, h):
        self.h = h
        self.n = 0

    def bump(self, k=1):
        self.n += k
        return self.n


class Arena:
    def __init__(self, handle, nbytes):
        self.h = handle
        self.nbytes = nbytes
        self.off = 0
        self.marks = []
        self.peak = 0

    def alloc(self, shape, dtype):
        esz = mybir.dt.size(dtype)
        n = int(np.prod(shape)) * esz
        off = (self.off + 15) & ~15
        assert off + n <= self.nbytes, f"arena overflow: {off}+{n} > {self.nbytes}"
        self.off = off + n
        self.peak = max(self.peak, self.off)
        ap = self.h.ap()[:, off:off + n].bitcast(dtype)
        if len(shape) > 1:
            names = " ".join(f"d{i}" for i in range(len(shape)))
            ap = ap.rearrange(f"p ({names}) -> p {names}",
                              **{f"d{i}": s for i, s in enumerate(shape)})
        return ap

    def push(self):
        self.marks.append(self.off)

    def pop(self):
        self.off = self.marks.pop()


def build_program():
    TTOT, DSL, TSL = _dims()
    DH = DSL // 128
    KDM = DM // 128
    KDI = DI // 128
    KH = HID // 128
    RND = L // C1
    NB1 = C1 // 512
    NTB1 = C1 // 128
    NJ2 = C1 // C2
    SUB = B * C1 // NCORE
    NTB3 = SUB // 128
    NGRP = NST // NG
    assert DSL % 128 == 0 and L % C1 == 0 and C1 % C2 == 0 and NJ2 == 2

    nc = bass.Bass("TRN2", target_bir_lowering=False, debug=False)

    d_x = nc.dram_tensor("x", [TTOT, DM], F32, kind="ExternalInput").ap()
    d_xmine = nc.dram_tensor("xmine", [TSL, DM], F32, kind="ExternalInput").ap()
    d_winT = nc.dram_tensor("winT", [DM, 2 * DSL], F32, kind="ExternalInput").ap()
    d_inb = nc.dram_tensor("inb", [128, 2 * DH], F32, kind="ExternalInput").ap()
    d_convw = nc.dram_tensor("convw", [128, DH * 4], F32, kind="ExternalInput").ap()
    d_convb = nc.dram_tensor("convb", [128, DH], F32, kind="ExternalInput").ap()
    d_xpwT = nc.dram_tensor("xpwT", [128, DH * XPW], F32, kind="ExternalInput").ap()
    d_dtwT = nc.dram_tensor("dtwT", [DTR, DSL], F32, kind="ExternalInput").ap()
    d_dtb = nc.dram_tensor("dtb", [128, DH], F32, kind="ExternalInput").ap()
    d_alog = nc.dram_tensor("alog", [128, DH * NST], F32, kind="ExternalInput").ap()
    d_dpar = nc.dram_tensor("dpar", [128, DH], F32, kind="ExternalInput").ap()
    d_owT = nc.dram_tensor("owT", [DI, DM], F32, kind="ExternalInput").ap()
    d_ob = nc.dram_tensor("ob", [1, DM], F32, kind="ExternalInput").ap()
    d_w1T = nc.dram_tensor("w1T", [HID, DM], F32, kind="ExternalInput").ap()
    d_b1 = nc.dram_tensor("b1", [128, KH], F32, kind="ExternalInput").ap()
    d_w2T = nc.dram_tensor("w2T", [HID, DM], F32, kind="ExternalInput").ap()
    d_b2 = nc.dram_tensor("b2", [1, DM], F32, kind="ExternalInput").ap()
    d_idf = nc.dram_tensor("idf", [128, 128], F32, kind="ExternalInput").ap()
    d_idb = nc.dram_tensor("idb", [128, 128], BF16, kind="ExternalInput").ap()

    d_out = nc.dram_tensor("out", [TSL, DM], F32, kind="ExternalOutput").ap()

    d_uscr = nc.dram_tensor("uscr", [DH, 128, TTOT], BF16, kind="Internal").ap()
    d_gscr = nc.dram_tensor("gscr", [DH, 128, TTOT], BF16, kind="Internal").ap()
    d_xp = nc.dram_tensor("xp", [RND * B * XPW, C1], F32,
                          kind="Internal").ap()
    d_xdbl = nc.dram_tensor("xdbl", [RND * B * XPW, C1], F32,
                            kind="Internal").ap()
    d_yg = nc.dram_tensor("yg", [RND * DI, SUB], BF16, kind="Internal").ap()
    d_ygx = nc.dram_tensor("ygx", [RND * DI, SUB], BF16, kind="Internal").ap()
    d_wq1 = nc.dram_tensor("wq1", [HID, DM], BF16, kind="Internal").ap()
    d_wq2 = nc.dram_tensor("wq2", [HID, DM], BF16, kind="Internal").ap()
    d_wqo = nc.dram_tensor("wqo", [DI, DM], BF16, kind="Internal").ap()
    d_bcb = nc.dram_tensor("bcb", [RND * B * NJ2 * 2 * NST, C2], BF16,
                           kind="Internal").ap()

    ENGS = ("sync", "gpsimd", "scalar", "vector", "tensor")
    prog = {e: [] for e in ENGS}
    engobj = {}

    es = contextlib.ExitStack()
    ARENA_BYTES = 212736
    arena_h = es.enter_context(
        nc.sbuf_tensor("arena", [128, ARENA_BYTES], mybir.dt.int8))
    A = Arena(arena_h, ARENA_BYTES)
    PS = [es.enter_context(nc.psum_tensor(f"ps{i}", [128, 512], F32))
          for i in range(8)]
    PSB = [p.ap().bitcast(BF16) for p in PS]

    def mksem(name):
        return Sem(es.enter_context(nc.semaphore(name)))
    CC = mksem("CC")
    PE = mksem("PE")
    ACT = mksem("ACT")
    VEC = mksem("VEC")
    D_P0L = mksem("D_P0L")
    D_P0S = mksem("D_P0S")
    D_WPL = [mksem("D_WPL0"), mksem("D_WPL1")]
    D_WPS = [mksem("D_WPS0"), mksem("D_WPS1")]
    D_X = [mksem("D_X0"), mksem("D_X1")]
    D_U = mksem("D_U")
    D_G = mksem("D_G")
    D_XPS = mksem("D_XPS")
    D_U2 = mksem("D_U2")
    D_XD = mksem("D_XD")
    D_BR = [mksem("D_BR0"), mksem("D_BR1")]
    D_CR = [mksem("D_CR0"), mksem("D_CR1")]
    D_BC = mksem("D_BC")
    D_YG = mksem("D_YG")
    D_YST = mksem("D_YST")
    D_XR = mksem("D_XR")
    D_W1 = [mksem(f"D_W1{i}") for i in range(4)]
    D_W2 = [mksem(f"D_W2{i}") for i in range(4)]
    D_WO = [mksem("D_WOA"), mksem("D_WOB")]
    D_OUT = mksem("D_OUT")
    ALLSEMS = (CC, PE, ACT, VEC, D_P0L, D_P0S,
               D_WPL[0], D_WPL[1], D_WPS[0], D_WPS[1],
               D_X[0], D_X[1], D_U, D_G,
               D_XPS,
               D_U2, D_XD, D_BR[0], D_BR[1], D_CR[0], D_CR[1], D_BC,
               D_YG, D_YST, D_XR, *D_W1, *D_W2,
               D_WO[0], D_WO[1], D_OUT)

    def step(eng, waits=(), inc=None, inck=1):
        wl = [(s.h, v) for (s, v) in waits if v > 0]
        base = inc.n if inc is not None else None
        if inc is not None:
            inc.bump(inck)

        def deco(fn):
            def run():
                E = engobj[eng]
                for h, v in wl:
                    E.wait_ge(h, v)
                if inc is None:
                    fn(E, None, None)
                    return
                st = {"c": base}

                def link(inst, k=1, wait=True):
                    st["c"] += k
                    assert st["c"] < base + inck, "link() overran inck"
                    inst.then_inc(inc.h, k)
                    if wait:
                        E.wait_ge(inc.h, st["c"])

                def fin(inst):
                    rem = (base + inck) - st["c"]
                    st["c"] = base + inck
                    inst.then_inc(inc.h, rem)
                fn(E, link, fin)
                assert st["c"] == base + inck, "fin() not called"
            prog[eng].append(run)
            return None
        return deco

    def snap():
        return [(s, s.n) for s in ALLSEMS]

    def join(snapshot):
        for e in ENGS:
            @step(e, waits=snapshot)
            def _(E, link, fin):
                pass

    def col(ap2d, i):
        return ap2d[:, i:i + 1]

    bankrd = {i: (VEC, 0) for i in range(8)}

    t_win = A.alloc([KDM, 2 * DSL], BF16)
    t_inb = A.alloc([2 * DH], F32)
    t_xpw = A.alloc([DH, XPW], BF16)
    t_dtw = A.alloc([DSL], BF16)
    t_dtb = A.alloc([DH], F32)
    t_convw = A.alloc([DH, 4], F32)
    t_convb = A.alloc([DH], F32)
    t_a = A.alloc([DH, NST], F32)
    t_dpar = A.alloc([DH], F32)
    t_obrow = A.alloc([DM], F32)
    t_b2row = A.alloc([DM], F32)
    t_b1q16 = A.alloc([KH], F32)
    t_idf = A.alloc([128], F32)
    t_idb = A.alloc([128], BF16)
    t_hst2 = A.alloc([B * DH, NST], F32)
    t_consts = A.alloc([8], F32)
    C_ONE, C_M1, C_M05, C_EPS, C_I1024 = 0, 1, 2, 3, 4

    A.push()
    stg = A.alloc([KDM * 2 * DSL], F32)
    stg2 = A.alloc([HID], F32)
    stgq = A.alloc([HID], F32)
    stgi = A.alloc([HID], I32)
    stgi2 = A.alloc([HID], I32)
    stgb = A.alloc([HID], BF16)

    @step("sync", inc=D_P0L, inck=16 * 10)
    def _(E, link, fin):
        for dst, src in [
            (stg.rearrange("p (a b) -> p a b", a=KDM),
             d_winT.rearrange("(a p) b -> p a b", p=128)),
            (t_inb, d_inb),
            (t_convw, d_convw.rearrange("p (a b) -> p a b", a=DH)),
            (t_convb, d_convb),
            (t_dtb, d_dtb),
            (t_dpar, d_dpar),
            (t_idf, d_idf),
            (t_idb, d_idb),
        ]:
            link(E.dma_start(out=dst, in_=src), 16, wait=False)
        link(E.dma_start(out=t_obrow,
                         in_=d_ob[None, 0, :].to_broadcast([128, DM])),
             16, wait=False)
        fin(E.dma_start(out=t_b2row,
                        in_=d_b2[None, 0, :].to_broadcast([128, DM])))
    ld0 = D_P0L.n

    @step("vector", waits=[(D_P0L, ld0)], inc=VEC)
    def _(E, link, fin):
        for i, v in [(C_ONE, 1.0), (C_M1, -1.0), (C_M05, -0.5),
                     (C_EPS, LN_EPS), (C_I1024, 1.0 / 1024.0)]:
            E.memset(t_consts[:, i:i + 1], v)
        E.memset(t_hst2.rearrange("p a b -> p (a b)"), 0.0)
        fin(E.tensor_copy(t_win.rearrange("p a b -> p (a b)"), stg))
    v_cast_win = VEC.n

    @step("sync", waits=[(VEC, v_cast_win)], inc=D_P0L, inck=16 * 3)
    def _(E, link, fin):
        link(E.dma_start(out=stg[:, :DH * XPW], in_=d_xpwT), 16, wait=False)
        link(E.dma_start(out=stg[0:DTR, DH * XPW:DH * XPW + DSL], in_=d_dtwT),
             16, wait=False)
        fin(E.dma_start(
            out=stg[:, DH * XPW + DSL:DH * XPW + DSL + DH * NST], in_=d_alog))
    ld1 = D_P0L.n

    @step("scalar", waits=[(D_P0L, ld1)], inc=ACT)
    def _(E, link, fin):
        fin(E.activation(t_a.rearrange("p a b -> p (a b)"),
                         stg[:, DH * XPW + DSL:DH * XPW + DSL + DH * NST],
                         AF.Exp))
    a_exp = ACT.n

    @step("vector", waits=[(ACT, a_exp)], inc=VEC)
    def _(E, link, fin):
        a_flat = t_a.rearrange("p a b -> p (a b)")
        E.tensor_scalar_mul(a_flat, a_flat, -1.0)
        E.tensor_copy(t_xpw.rearrange("p a b -> p (a b)"), stg[:, :DH * XPW])
        fin(E.tensor_copy(t_dtw[0:DTR, :],
                          stg[0:DTR, DH * XPW:DH * XPW + DSL]))

    def emit_rne(E, link, src_f32, n, scale):
        link(E.tensor_scalar(stgq[:, :n], src_f32, float(scale), 256.0,
                             ALU.mult, ALU.add))
        link(E.tensor_copy(stgi[:, :n], stgq[:, :n]))
        link(E.tensor_sub(stg2[:, :n], stgq[:, :n], stgi[:, :n]))
        link(E.tensor_single_scalar(stg2[:, :n], stg2[:, :n], 0.5, ALU.is_gt))
        link(E.tensor_add(stgi2[:, :n], stgi[:, :n], stg2[:, :n]))
        link(E.tensor_scalar(stgi2[:, :n], stgi2[:, :n], 256, -128,
                             ALU.subtract, ALU.max))

    @step("sync", waits=[(VEC, VEC.n)], inc=D_P0L, inck=16)
    def _(E, link, fin):
        fin(E.dma_start(out=stg[:, :KH], in_=d_b1))
    ldb1 = D_P0L.n

    @step("vector", waits=[(D_P0L, ldb1)], inc=VEC, inck=8)
    def _(E, link, fin):
        emit_rne(E, link, stg[:, :KH], KH, 64.0)
        link(E.tensor_single_scalar(stgi2[:, :KH], stgi2[:, :KH], 127, ALU.min))
        fin(E.tensor_scalar_mul(t_b1q16, stgi2[:, :KH], 16.0))

    @step("vector", waits=[(VEC, VEC.n)], inc=VEC, inck=8)
    def _(E, link, fin):
        emit_rne(E, link, t_b2row, DM, 64.0)
        link(E.tensor_single_scalar(stgi2[:, :DM], stgi2[:, :DM], 127, ALU.min))
        fin(E.tensor_scalar_mul(t_b2row, stgi2[:, :DM], 1.0 / 64.0))

    w1_src = d_w1T.rearrange("(a p) b -> p a b", p=128)
    wq1_dst = d_wq1.rearrange("(a p) b -> p a b", p=128)
    w2_src = d_w2T.rearrange("(a p) b -> p a b", p=128)
    wq2_dst = d_wq2.rearrange("(a p) b -> p a b", p=128)
    wo_src = d_owT.rearrange("(a p) b -> p a b", p=128)
    wqo_dst = d_wqo.rearrange("(a p) b -> p a b", p=128)
    A.pop()

    wprep_done = {}

    def emit_WPREP():
        units = []
        for kind, src, dst, ntile, quant in [
                ("wo", wo_src, wqo_dst, KDI, False),
                ("w1", w1_src, wq1_dst, KH, True),
                ("w2", w2_src, wq2_dst, KH, True)]:
            for t in range(ntile):
                for h in range(2):
                    units.append((kind, src, dst, t, h, quant))
        for u, (kind, src, dst, t, h, quant) in enumerate(units):
            sl = u % 2

            @step("sync", waits=[(D_WPS[sl], D_WPS[sl].n)],
                  inc=D_WPL[sl], inck=16)
            def _(E, link, fin, src=src, t=t, h=h, sl=sl):
                fin(E.dma_start(out=t_ws[sl],
                                in_=src[:, t, h * 512:(h + 1) * 512]))
            ldw = D_WPL[sl].n

            if quant:
                @step("vector", waits=[(D_WPL[sl], ldw),
                                       (D_WPS[sl], D_WPS[sl].n)],
                      inc=VEC, inck=7)
                def _(E, link, fin, sl=sl):
                    link(E.tensor_scalar(t_wq, t_ws[sl], 64.0, 256.0,
                                         ALU.mult, ALU.add))
                    link(E.tensor_copy(t_wi, t_wq))
                    link(E.tensor_sub(t_wq, t_wq, t_wi))
                    link(E.tensor_single_scalar(t_wq, t_wq, 0.5, ALU.is_gt))
                    link(E.tensor_add(t_wi2, t_wi, t_wq))
                    link(E.tensor_scalar(t_wi2, t_wi2, 256, -128,
                                         ALU.subtract, ALU.max))
                    fin(E.tensor_single_scalar(t_wb[sl], t_wi2, 127, ALU.min))
            else:
                @step("vector", waits=[(D_WPL[sl], ldw),
                                       (D_WPS[sl], D_WPS[sl].n)],
                      inc=VEC)
                def _(E, link, fin, sl=sl):
                    fin(E.tensor_copy(t_wb[sl], t_ws[sl]))
            vq = VEC.n

            @step("scalar", waits=[(VEC, vq)], inc=D_WPS[sl], inck=16)
            def _(E, link, fin, dst=dst, t=t, h=h, sl=sl):
                fin(E.dma_start(out=dst[:, t, h * 512:(h + 1) * 512],
                                in_=t_wb[sl]))
            wprep_done.setdefault((kind, t), []).append(
                (D_WPS[sl], D_WPS[sl].n))
            yield
    join(snap())

    t_xstg = [A.alloc([DM], F32) for _ in range(2)]
    t_xT = A.alloc([KDM, C1], BF16)
    t_xbraw = A.alloc([DH, C1 + 4], F32)
    t_tail = A.alloc([B, DH, 4], F32)
    t_acc_a = A.alloc([C1], F32)
    t_acc_b = A.alloc([C1], F32)
    t_u1b = A.alloc([DH, C1], BF16)
    t_g1 = A.alloc([DH, C1], BF16)
    t_xp1 = t_acc_a
    t_u2 = [A.alloc([DH, C2], BF16) for _ in range(2)]
    t_g2 = [A.alloc([DH, C2], BF16) for _ in range(2)]
    _xds = A.alloc([C2], F32)
    t_xd = [_xds, _xds]
    t_xdb = A.alloc([C2], BF16)
    t_z2 = A.alloc([DH, C2], F32)
    t_az = A.alloc([DH, C2], F32)
    t_delta = A.alloc([DH, C2], F32)
    t_du = A.alloc([DH, C2], BF16)
    t_brep = [A.alloc([NG, C2], BF16) for _ in range(2)]
    t_crep = [A.alloc([NG, C2], BF16) for _ in range(2)]
    t_dA = [A.alloc([NG, C2], F32) for _ in range(3)]
    t_dBu = A.alloc([NG, C2], BF16)
    t_h2 = A.alloc([NG, C2], BF16)
    t_y2 = t_z2
    t_yg2 = A.alloc([DH, C2], BF16)
    t_yT = A.alloc([KDI, SUB], BF16)
    t_wos = [A.alloc([DM], BF16) for _ in range(2)]
    t_v3 = A.alloc([DM], F32)
    t_xres = A.alloc([DM], F32)
    t_scr = t_v3
    t_stats = A.alloc([8], F32)
    S_S1, S_S2, S_MU, S_VAR, S_RSTD = 0, 1, 2, 3, 4
    t_ln1save = [A.alloc([NTB3, DM], F32) for _ in range(2)]
    t_qi = A.alloc([DM], I32)
    t_xqb = A.alloc([DM], BF16)
    t_x1T = A.alloc([KDM, SUB], BF16)
    t_w1s = [A.alloc([KDM, 128], BF16) for _ in range(4)]
    t_h1q = A.alloc([KH, SUB], BF16)
    t_w2s = [A.alloc([DM], BF16) for _ in range(4)]
    t_ffn = A.alloc([DM], F32)
    t_ws = [A.alloc([512], F32) for _ in range(2)]
    t_wq = A.alloc([512], F32)
    t_wi = t_qi[:, 0:512]
    t_wi2 = t_qi[:, 512:1024]
    t_wb = [A.alloc([512], BF16) for _ in range(2)]
    t_qf = t_ffn
    t_mi = [t_qi[:, 0:SUB], t_qi[:, 2 * SUB:3 * SUB]]
    t_mb = [t_qi[:, SUB:2 * SUB], t_qi[:, 3 * SUB:4 * SUB]]
    t_ms = t_mb
    assert A.peak <= A.nbytes, A.peak

    lc = {}

    def lcw(key):
        return lc.get(key, (VEC, 0))

    cc1_done = {}
    cc2_done = {}
    xp_stored = {}
    ug_stored = {}
    yg_stored = {}

    def emit_A1a(bt, r):
        tok0 = bt * L + r * C1
        ldx_of = {}

        def xload(tb):
            xp_par = tb % 2

            @step("scalar", waits=[lcw(("xstg", xp_par))], inc=D_X[xp_par],
                  inck=16)
            def _(E, link, fin, tok0=tok0, tb=tb, xp_par=xp_par):
                fin(E.dma_start(
                    out=t_xstg[xp_par],
                    in_=d_x[tok0 + tb * 128: tok0 + (tb + 1) * 128, :]))
            ldx_of[tb] = D_X[xp_par].n

        def xtrans(tb):
            xp_par = tb % 2
            ldx = ldx_of[tb]
            for g in range(2):
                bank = 4 + g
                @step("tensor", waits=[(D_X[xp_par], ldx), bankrd[bank]],
                      inc=PE)
                def _(E, link, fin, g=g, bank=bank, xp_par=xp_par):
                    for j in range(4):
                        kb = g * 4 + j
                        inst = E.transpose(
                            PS[bank][:, j * 128:(j + 1) * 128],
                            t_xstg[xp_par][:, kb * 128:(kb + 1) * 128],
                            t_idf)
                    fin(inst)
                pet = PE.n
                lc[("xstg", xp_par)] = (PE, PE.n)

                @step("scalar", waits=[(PE, pet)], inc=ACT)
                def _(E, link, fin, g=g, tb=tb, bank=bank):
                    for j in range(4):
                        kb = g * 4 + j
                        inst = E.copy(t_xT[:, kb, tb * 128:(tb + 1) * 128],
                                      PS[bank][:, j * 128:(j + 1) * 128])
                    fin(inst)
                bankrd[bank] = (ACT, ACT.n)

        for it in range(NTB1 + 2):
            if it >= 2:
                xtrans(it - 2)
            if it < NTB1:
                xload(it)
            if it >= 2:
                yield
        act_xt = ACT.n

        for s in range(NB1):
            for pb in range(4):
                bank = 6 + (pb % 2)
                @step("tensor", waits=[(ACT, act_xt), bankrd[bank]],
                      inc=PE)
                def _(E, link, fin, s=s, pb=pb, bank=bank):
                    for kb in range(KDM):
                        inst = E.matmul(
                            PS[bank][:, :512],
                            t_win[:, kb, pb * 128:(pb + 1) * 128],
                            t_xT[:, kb, s * 512:(s + 1) * 512],
                            start=(kb == 0), stop=(kb == KDM - 1))
                    fin(inst)
                pei = PE.n

                @step("scalar", waits=[(PE, pei), lcw(("a1evac", pb))],
                      inc=ACT)
                def _(E, link, fin, s=s, pb=pb, bank=bank):
                    if pb < 2:
                        fin(E.activation(
                            t_xbraw[:, pb, 3 + s * 512: 3 + (s + 1) * 512],
                            PS[bank][:, :512], AF.Identity,
                            bias=col(t_inb, pb)))
                    else:
                        fin(E.activation(
                            t_g1[:, pb - 2, s * 512:(s + 1) * 512],
                            PS[bank][:, :512], AF.Silu,
                            bias=col(t_inb, DH + pb - 2)))
                bankrd[bank] = (ACT, ACT.n)
                yield

    def emit_A1b(bt, r):
        tok0 = bt * L + r * C1
        act_inproj = ACT.n
        for dh in range(DH):
            @step("vector",
                  waits=[(ACT, act_inproj), (VEC, VEC.n),
                         (D_XPS, D_XPS.n), lcw(("accb",)), lcw(("acca",))],
                  inc=VEC, inck=7)
            def _(E, link, fin, dh=dh, bt=bt, r=r):
                src = t_xbraw[:, dh, :]
                if r == 0:
                    link(E.memset(src[:, 0:3], 0.0))
                else:
                    link(E.tensor_copy(src[:, 0:3], t_tail[:, bt, dh, 0:3]))
                link(E.tensor_scalar(t_acc_a, src[:, 0:C1],
                                     col(t_convw[:, dh, :], 0), None,
                                     ALU.mult))
                link(E.scalar_tensor_tensor(t_acc_b, src[:, 1:C1 + 1],
                                            col(t_convw[:, dh, :], 1),
                                            t_acc_a, ALU.mult, ALU.add))
                link(E.scalar_tensor_tensor(t_acc_a, src[:, 2:C1 + 2],
                                            col(t_convw[:, dh, :], 2),
                                            t_acc_b, ALU.mult, ALU.add))
                link(E.scalar_tensor_tensor(t_acc_b, src[:, 3:C1 + 3],
                                            col(t_convw[:, dh, :], 3),
                                            t_acc_a, ALU.mult, ALU.add))
                fin(E.tensor_copy(t_tail[:, bt, dh, 0:3], src[:, C1:C1 + 3]))
            vconv = VEC.n
            lc[("a1evac", dh)] = (VEC, vconv)

            @step("scalar", waits=[(VEC, vconv), lcw(("u1b",)),
                                   (D_U, D_U.n)],
                  inc=ACT)
            def _(E, link, fin, dh=dh):
                fin(E.activation(t_u1b[:, dh, :], t_acc_b, AF.Silu,
                                 bias=col(t_convb, dh)))
            lc[("accb",)] = (ACT, ACT.n)
            yield
        act_u = ACT.n

        for s in range(NB1):
            bank = 6
            @step("tensor", waits=[(ACT, act_u), bankrd[bank]], inc=PE)
            def _(E, link, fin, s=s, bank=bank):
                for dh in range(DH):
                    inst = E.matmul(PS[bank][0:XPW, :512],
                                    t_xpw[:, dh, :],
                                    t_u1b[:, dh, s * 512:(s + 1) * 512],
                                    start=(dh == 0), stop=(dh == DH - 1))
                fin(inst)
            pex = PE.n

            @step("scalar", waits=[(PE, pex), lcw(("xp1",))], inc=ACT)
            def _(E, link, fin, s=s, bank=bank):
                fin(E.copy(t_xp1[0:XPW, s * 512:(s + 1) * 512],
                           PS[bank][0:XPW, :512]))
            bankrd[bank] = (ACT, ACT.n)
            yield
        act_xp = ACT.n
        lc[("u1b",)] = (PE, PE.n)
        lc[("acca",)] = (ACT, act_xp)

        @step("sync", waits=[(ACT, act_xp)], inc=D_XPS, inck=16)
        def _(E, link, fin, r=r, bt=bt):
            rb0 = (r * B + bt) * XPW
            fin(E.dma_start(out=d_xp[rb0:rb0 + XPW, :],
                            in_=t_xp1[0:XPW, :]))
        lc[("xp1",)] = (D_XPS, D_XPS.n)

        @step("sync", inc=D_U, inck=32)
        def _(E, link, fin, tok0=tok0):
            link(E.dma_start(
                out=d_uscr[:, :, tok0:tok0 + C1].rearrange("a p c -> p a c"),
                in_=t_u1b), 16, wait=False)
            fin(E.dma_start(
                out=d_gscr[:, :, tok0:tok0 + C1].rearrange("a p c -> p a c"),
                in_=t_g1))
        lc[("a1evac", 2)] = (D_U, D_U.n)
        lc[("a1evac", 3)] = (D_U, D_U.n)
        ug_stored[(bt, r)] = (D_U, D_U.n)
        xp_stored[(bt, r)] = (D_XPS, D_XPS.n)

        @step("gpsimd", waits=[xp_stored[(bt, r)]], inc=CC)
        def _(E, link, fin, r=r, bt=bt):
            rb0 = (r * B + bt) * XPW
            fin(E.collective_compute(
                "AllReduce", ALU.add, replica_groups=[list(range(NCORE))],
                ins=[d_xp[rb0:rb0 + XPW, :]],
                outs=[d_xdbl[rb0:rb0 + XPW, :]]))
        cc1_done[(r, bt)] = CC.n
        yield


    def emit_A2(bt, r, j, P):
        tok0 = bt * L + r * C1 + j * C2
        cc1v = cc1_done[(r, bt)]

        @step("gpsimd",
              waits=[ug_stored[(bt, r)], lcw(("u2", P)),
                     lcw(("g2", P))],
              inc=D_U2, inck=32)
        def _(E, link, fin, tok0=tok0, P=P):
            link(E.dma_start(out=t_u2[P], in_=d_uscr[:, :, tok0:tok0 + C2]
                             .rearrange("a p c -> p a c")), 16, wait=False)
            fin(E.dma_start(out=t_g2[P], in_=d_gscr[:, :, tok0:tok0 + C2]
                            .rearrange("a p c -> p a c")))
        ldug = D_U2.n

        @step("gpsimd", waits=[(CC, cc1v), lcw(("xd", P))], inc=D_XD,
              inck=16)
        def _(E, link, fin, r=r, bt=bt, j=j, P=P):
            rb0 = (r * B + bt) * XPW
            fin(E.dma_start(out=t_xd[P][0:XPW, :],
                            in_=d_xdbl[rb0:rb0 + XPW,
                                       j * C2:(j + 1) * C2]))
        ldxd = D_XD.n

        @step("vector", waits=[(D_XD, ldxd), (VEC, VEC.n), lcw(("xdb",)),
                               lcw(("xdbst",))],
              inc=VEC)
        def _(E, link, fin, P=P):
            fin(E.tensor_copy(t_xdb[0:XPW, :], t_xd[P][0:XPW, :]))
        vxdb = VEC.n
        lc[("xd", P)] = (VEC, vxdb)

        bcrow0 = ((r * B + bt) * NJ2 + j) * 2 * NST

        @step("gpsimd", waits=[(VEC, vxdb)], inc=D_BC, inck=16)
        def _(E, link, fin, bcrow0=bcrow0):
            fin(E.dma_start(out=d_bcb[bcrow0:bcrow0 + 2 * NST, :],
                            in_=t_xdb[DTR:XPW, :]))
        lc[("xdbst",)] = (D_BC, D_BC.n)
        bcv = D_BC.n
        yield

        @step("tensor", waits=[(VEC, vxdb), bankrd[4], bankrd[5]], inc=PE)
        def _(E, link, fin):
            for pb in range(DH):
                inst = E.matmul(PS[4 + pb][:, :C2],
                                t_dtw[0:DTR, pb * 128:(pb + 1) * 128],
                                t_xdb[0:DTR, :],
                                start=True, stop=True)
            fin(inst)
        pedt = PE.n
        lc[("xdb",)] = (PE, pedt)

        @step("scalar", waits=[(PE, pedt), lcw(("z2",)), lcw(("az",))],
              inc=ACT, inck=4)
        def _(E, link, fin):
            for pb in range(DH):
                link(E.activation(t_z2[:, pb, :], PS[4 + pb][:, :C2],
                                  AF.Identity, bias=col(t_dtb, pb)))
            for pb in range(DH):
                inst = E.activation(t_az[:, pb, :], PS[4 + pb][:, :C2],
                                    AF.Abs, bias=col(t_dtb, pb))
                if pb < DH - 1:
                    link(inst)
            fin(inst)
        bankrd[4] = (ACT, ACT.n)
        bankrd[5] = (ACT, ACT.n)

        @step("scalar", waits=[(ACT, ACT.n)], inc=ACT, inck=2)
        def _(E, link, fin):
            azf = t_az.rearrange("p a b -> p (a b)")
            link(E.activation(azf, azf, AF.Exp, scale=-1.0))
            fin(E.activation(azf, azf, AF.Ln, bias=col(t_consts, C_ONE)))
        actsp = ACT.n
        yield

        for dh in range(DH):
            @step("vector",
                  waits=[(ACT, actsp), (D_U2, ldug), lcw(("dadu", dh))],
                  inc=VEC, inck=2)
            def _(E, link, fin, dh=dh):
                link(E.scalar_tensor_tensor(t_delta[:, dh, :],
                                            t_z2[:, dh, :], 0.0,
                                            t_az[:, dh, :],
                                            ALU.max, ALU.add))
                fin(E.tensor_mul(t_du[:, dh, :], t_delta[:, dh, :],
                                 t_u2[P][:, dh, :]))
            lc[("az",)] = (VEC, VEC.n)
            yield
        vdu = VEC.n

        groups = [(g, dh) for g in range(NGRP) for dh in range(DH)]
        NGT = len(groups)
        ld_br = {}
        ld_cr = {}

        def emit_load(i):
            g = groups[i][0]
            gp = g % 2

            @step("scalar", waits=[(D_BC, bcv), lcw(("brep", gp))],
                  inc=D_BR[gp], inck=16)
            def _(E, link, fin, g=g, gp=gp, bcrow0=bcrow0):
                r0 = bcrow0 + g * NG
                fin(E.dma_start(
                    out=t_brep[gp],
                    in_=d_bcb[None, r0:r0 + NG, :]
                    .to_broadcast([128, NG, C2])))
            ld_br[g] = D_BR[gp].n

            @step("gpsimd", waits=[(D_BC, bcv), lcw(("crep", gp))],
                  inc=D_CR[gp], inck=16)
            def _(E, link, fin, g=g, gp=gp, bcrow0=bcrow0):
                r0 = bcrow0 + NST + g * NG
                fin(E.dma_start(
                    out=t_crep[gp],
                    in_=d_bcb[None, r0:r0 + NG, :]
                    .to_broadcast([128, NG, C2])))
            ld_cr[g] = D_CR[gp].n

        def emit_exp(i):
            g, dh = groups[i]
            ip = i % 3

            @step("scalar", waits=[(VEC, vdu), lcw(("dA", ip))], inc=ACT)
            def _(E, link, fin, dh=dh, g=g, ip=ip):
                for n in range(NG):
                    inst = E.activation(t_dA[ip][:, n, :],
                                        t_delta[:, dh, :], AF.Exp,
                                        scale=t_a[:, dh, g * NG + n:
                                                  g * NG + n + 1])
                fin(inst)
            return ACT.n

        actda_of = {}

        def emit_scan(i):
            g, dh = groups[i]
            gp = g % 2
            ip = i % 3
            actda = actda_of[i]

            @step("vector",
                  waits=[(ACT, actda), (D_BR[gp], ld_br[g]),
                         (D_CR[gp], ld_cr[g]), (VEC, VEC.n)],
                  inc=VEC, inck=6)
            def _(E, link, fin, dh=dh, g=g, gp=gp, ip=ip, bt=bt):
                link(E.tensor_mul(
                    t_dBu,
                    t_du[:, dh, :][:, None, :].to_broadcast([128, NG, C2]),
                    t_brep[gp]))
                for n in range(NG):
                    inst = E.tensor_tensor_scan(
                        t_h2[:, n, :], t_dA[ip][:, n, :], t_dBu[:, n, :],
                        t_hst2[:, bt * DH + dh, g * NG + n:g * NG + n + 1],
                        ALU.mult, ALU.add)
                link(inst)
                link(E.tensor_copy(
                    t_hst2[:, bt * DH + dh, g * NG:(g + 1) * NG],
                    t_h2[:, :, C2 - 1:C2].rearrange("p a b -> p (a b)")))
                link(E.tensor_mul(t_dBu, t_h2, t_crep[gp]))
                if g == 0:
                    fin(E.tensor_add(t_y2[:, dh, :], t_dBu[:, 0, :],
                                     t_dBu[:, 1, :]))
                else:
                    link(E.tensor_add(t_y2[:, dh, :], t_y2[:, dh, :],
                                      t_dBu[:, 0, :]))
                    fin(E.tensor_add(t_y2[:, dh, :], t_y2[:, dh, :],
                                     t_dBu[:, 1, :]))
            lc[("dA", ip)] = (VEC, VEC.n)
            if dh == DH - 1:
                lc[("brep", gp)] = (VEC, VEC.n)
                lc[("crep", gp)] = (VEC, VEC.n)
            lc[("dadu", dh)] = (VEC, VEC.n)
            if g == NGRP - 1:
                @step("vector", waits=[(VEC, VEC.n), lcw(("yg2",))],
                      inc=VEC, inck=2)
                def _(E, link, fin, dh=dh, P=P):
                    link(E.scalar_tensor_tensor(t_y2[:, dh, :],
                                                t_u2[P][:, dh, :],
                                                col(t_dpar, dh),
                                                t_y2[:, dh, :],
                                                ALU.mult, ALU.add))
                    fin(E.tensor_mul(t_yg2[:, dh, :], t_y2[:, dh, :],
                                     t_g2[P][:, dh, :]))
                lc[("z2",)] = (VEC, VEC.n)

        for i in range(NGT + 2):
            if i >= 2:
                emit_scan(i - 2)
            if i < NGT:
                actda_of[i] = emit_exp(i)
            if i < NGT and groups[i][1] == 0:
                emit_load(i)
            if i >= 2:
                yield
        vyg = VEC.n
        lc[("u2", P)] = (VEC, vyg)
        lc[("g2", P)] = (VEC, vyg)

        NDD = C2 // SUB

        @step("sync", waits=[(VEC, vyg)], inc=D_YG, inck=16 * NDD)
        def _(E, link, fin, r=r, bt=bt, j=j):
            dest0 = (bt * C1 + j * C2) // SUB
            for dd in range(NDD):
                row0 = r * DI + (dest0 + dd) * DSL
                inst = E.dma_start(
                    out=d_yg[row0:row0 + DSL, :]
                    .rearrange("(a p) c -> p a c", p=128),
                    in_=t_yg2[:, :, dd * SUB:(dd + 1) * SUB])
                if dd < NDD - 1:
                    link(inst, 16, wait=False)
            fin(inst)
        lc[("yg2",)] = (D_YG, D_YG.n)
        yg_stored[(bt, r, j)] = (D_YG, D_YG.n)

    def emit_A3a(r):
        cc2v = cc2_done[r]
        @step("sync", waits=[(CC, cc2v), lcw(("yT",))],
              inc=D_YST, inck=16)
        def _(E, link, fin, r=r):
            fin(E.dma_start(
                out=t_yT,
                in_=d_ygx[r * DI:(r + 1) * DI, :]
                .rearrange("(a p) c -> p a c", p=128)))
        ldyT = D_YST.n
        yield

        for kb in range(KDI):
            wpar = kb % 2
            prev = lc.get(("wo", wpar), (VEC, 0))
            @step("sync", waits=[prev] + wprep_done[("wo", kb)],
                  inc=D_WO[wpar], inck=16)
            def _(E, link, fin, kb=kb, wpar=wpar):
                fin(E.dma_start(out=t_wos[wpar],
                                in_=wqo_dst[:, kb, :]))
            ldwo = D_WO[wpar].n

            waits = [(D_YST, ldyT), (D_WO[wpar], ldwo)]
            if kb == 0:
                waits += [bankrd[i] for i in range(4)]
            @step("tensor", waits=waits, inc=PE)
            def _(E, link, fin, kb=kb, wpar=wpar):
                for tb in range(NTB3):
                    for half in range(2):
                        inst = E.matmul(
                            PS[tb * 2 + half][:, :512],
                            t_yT[:, kb, tb * 128:(tb + 1) * 128],
                            t_wos[wpar][:, half * 512:(half + 1) * 512],
                            start=(kb == 0), stop=(kb == KDI - 1))
                fin(inst)
            lc[("wo", wpar)] = (PE, PE.n)
            yield
        peop = PE.n
        lc[("yT",)] = (PE, peop)

        for tb in range(NTB3):
            @step("sync", waits=[lcw(("xres",))], inc=D_XR, inck=16)
            def _(E, link, fin, r=r, tb=tb):
                r0 = r * SUB + tb * 128
                fin(E.dma_start(out=t_xres, in_=d_xmine[r0:r0 + 128, :]))
            ldr = D_XR.n

            @step("vector", waits=[(PE, peop), (D_XR, ldr), (VEC, VEC.n),
                                   (D_OUT, D_OUT.n), lcw(("v3",))],
                  inc=VEC, inck=3)
            def _(E, link, fin, tb=tb):
                link(E.tensor_add(t_v3[:, 0:512], PS[tb * 2][:, :512],
                                  t_obrow[:, 0:512]))
                link(E.tensor_add(t_v3[:, 512:1024], PS[tb * 2 + 1][:, :512],
                                  t_obrow[:, 512:1024]))
                fin(E.tensor_add(t_v3, t_v3, t_xres))
            vv = VEC.n
            bankrd[tb * 2] = (VEC, vv)
            bankrd[tb * 2 + 1] = (VEC, vv)
            lc[("xres",)] = (VEC, vv)

            @step("scalar", waits=[(VEC, vv), (D_OUT, D_OUT.n)],
                  inc=ACT, inck=2)
            def _(E, link, fin):
                link(E.activation(t_ffn, t_v3, AF.Copy,
                                  accum_out=col(t_stats, S_S1)))
                fin(E.activation(t_ffn, t_v3, AF.Square,
                                 accum_out=col(t_stats, S_S2)))
            actst = ACT.n

            @step("vector", waits=[(ACT, actst)], inc=VEC, inck=3)
            def _(E, link, fin):
                link(E.tensor_scalar_mul(col(t_stats, S_MU),
                                         col(t_stats, S_S1), 1.0 / DM))
                link(E.tensor_mul(col(t_stats, S_VAR), col(t_stats, S_MU),
                                  col(t_stats, S_MU)))
                fin(E.scalar_tensor_tensor(col(t_stats, S_VAR),
                                           col(t_stats, S_S2), 1.0 / DM,
                                           col(t_stats, S_VAR),
                                           ALU.mult, ALU.subtract))
            vvar = VEC.n

            @step("scalar", waits=[(VEC, vvar)], inc=ACT, inck=2)
            def _(E, link, fin):
                link(E.activation(col(t_stats, S_RSTD), col(t_stats, S_VAR),
                                  AF.Ln, bias=col(t_consts, C_EPS)))
                fin(E.activation(col(t_stats, S_RSTD), col(t_stats, S_RSTD),
                                 AF.Exp, scale=col(t_consts, C_M05)))
            actrs = ACT.n

            @step("vector", waits=[(ACT, actrs)], inc=VEC, inck=9)
            def _(E, link, fin, tb=tb):
                link(E.tensor_scalar(t_scr, t_v3, col(t_stats, S_MU),
                                     col(t_stats, S_RSTD), ALU.subtract,
                                     ALU.mult))
                link(E.tensor_copy(t_ln1save[r % 2][:, tb, :], t_scr))
                link(E.tensor_scalar(t_qf, t_scr, 16.0, 256.0, ALU.mult,
                                     ALU.add))
                link(E.tensor_copy(t_qi, t_qf))
                link(E.tensor_sub(t_qf, t_qf, t_qi))
                link(E.tensor_single_scalar(t_qf, t_qf, 0.5, ALU.is_gt))
                link(E.tensor_add(t_qi, t_qi, t_qf))
                link(E.tensor_scalar(t_qi, t_qi, 256, -128, ALU.subtract,
                                     ALU.max))
                fin(E.tensor_single_scalar(t_xqb, t_qi, 127, ALU.min))
            vxq = VEC.n
            lc[("v3",)] = (VEC, vxq)
            lc[("qi",)] = (VEC, vxq)
            yield

            for g in range(2):
                bank = 4 + g
                @step("tensor", waits=[(VEC, vxq), bankrd[bank]], inc=PE)
                def _(E, link, fin, g=g, bank=bank):
                    for j in range(4):
                        kb = g * 4 + j
                        inst = E.transpose(
                            PSB[bank][:, j * 128:(j + 1) * 128],
                            t_xqb[:, kb * 128:(kb + 1) * 128], t_idb)
                    fin(inst)
                pexq = PE.n

                @step("scalar", waits=[(PE, pexq)], inc=ACT)
                def _(E, link, fin, g=g, tb=tb, bank=bank):
                    for j in range(4):
                        kb = g * 4 + j
                        inst = E.copy(t_x1T[:, kb, tb * 128:(tb + 1) * 128],
                                      PSB[bank][:, j * 128:(j + 1) * 128])
                    fin(inst)
                bankrd[bank] = (ACT, ACT.n)
                lc[("x1T",)] = (ACT, ACT.n)
                yield

    def emit_A3b(r):
        act_x1t = lc.get(("x1T",), (ACT, 0))[1]
        for hb in range(KH):
            wpar = hb % 4
            bank = 6 + hb % 2
            prev = lc.get(("w1", wpar), (VEC, 0))
            @step("sync", waits=[prev] + wprep_done[("w1", hb)],
                  inc=D_W1[wpar], inck=16)
            def _(E, link, fin, hb=hb, wpar=wpar):
                src = d_wq1[hb * 128:(hb + 1) * 128, :]
                fin(E.dma_start(
                    out=t_w1s[wpar].rearrange("p a b -> p (a b)"), in_=src))
            ldw1 = D_W1[wpar].n

            @step("tensor",
                  waits=[(ACT, act_x1t), (D_W1[wpar], ldw1), bankrd[bank]],
                  inc=PE)
            def _(E, link, fin, hb=hb, wpar=wpar, bank=bank):
                for kb in range(KDM):
                    inst = E.matmul(PS[bank][:, :SUB],
                                    t_w1s[wpar][:, kb, :],
                                    t_x1T[:, kb, :],
                                    start=(kb == 0), stop=(kb == KDM - 1))
                fin(inst)
            lc[("w1", wpar)] = (PE, PE.n)
            peh = PE.n

            p2 = hb % 2

            @step("scalar", waits=[(PE, peh), lcw(("qi",)),
                                   lcw(("mi", p2))],
                  inc=ACT)
            def _(E, link, fin, hb=hb, bank=bank, p2=p2):
                fin(E.activation(t_mi[p2], PS[bank][:, :SUB], AF.Identity,
                                 bias=col(t_b1q16, hb)))
            actmi = ACT.n
            bankrd[bank] = (ACT, ACT.n)

            @step("vector", waits=[(ACT, actmi), (VEC, VEC.n),
                                   lcw(("h1q",))],
                  inc=VEC, inck=5)
            def _(E, link, fin, hb=hb, p2=p2):
                link(E.tensor_single_scalar(t_mb[p2], t_mi[p2], 6,
                                            ALU.arith_shift_right))
                link(E.tensor_single_scalar(t_mb[p2], t_mb[p2], 1,
                                            ALU.bitwise_and))
                link(E.scalar_tensor_tensor(t_ms[p2], t_mi[p2], 31,
                                            t_mb[p2], ALU.add, ALU.add))
                link(E.tensor_single_scalar(t_ms[p2], t_ms[p2], 6,
                                            ALU.arith_shift_right))
                fin(E.tensor_scalar(t_h1q[:, hb, :], t_ms[p2], 0, 127,
                                    ALU.max, ALU.min))
            lc[("mi", p2)] = (VEC, VEC.n)
            yield
        v_h1 = VEC.n

    def emit_A3c(r):
        v_h1 = VEC.n
        for kb in range(KH):
            wpar = kb % 4
            prev = lc.get(("w2", wpar), (VEC, 0))
            @step("sync", waits=[prev] + wprep_done[("w2", kb)],
                  inc=D_W2[wpar], inck=16)
            def _(E, link, fin, kb=kb, wpar=wpar):
                src = d_wq2.rearrange("(a p) b -> p a b", p=128)[:, kb, :]
                fin(E.dma_start(out=t_w2s[wpar], in_=src))
            ldw2 = D_W2[wpar].n

            waits = [(VEC, v_h1), (D_W2[wpar], ldw2)]
            if kb == 0:
                waits += [bankrd[i] for i in range(4)]
            @step("tensor", waits=waits, inc=PE)
            def _(E, link, fin, kb=kb, wpar=wpar):
                for tb in range(NTB3):
                    for half in range(2):
                        inst = E.matmul(
                            PS[tb * 2 + half][:, :512],
                            t_h1q[:, kb, tb * 128:(tb + 1) * 128],
                            t_w2s[wpar][:, half * 512:(half + 1) * 512],
                            start=(kb == 0), stop=(kb == KH - 1))
                fin(inst)
            lc[("w2", wpar)] = (PE, PE.n)
            yield
        pef2 = PE.n
        lc[("h1q",)] = (PE, pef2)

        for tb in range(NTB3):
            @step("scalar",
                  waits=[(PE, pef2), (VEC, VEC.n), (D_OUT, D_OUT.n)],
                  inc=ACT)
            def _(E, link, fin, tb=tb):
                E.activation(t_ffn[:, 0:512], PS[tb * 2][:, :512], AF.Copy,
                             scale=col(t_consts, C_I1024))
                fin(E.activation(t_ffn[:, 512:1024], PS[tb * 2 + 1][:, :512],
                                 AF.Copy, scale=col(t_consts, C_I1024)))
            actf = ACT.n
            bankrd[tb * 2] = (ACT, ACT.n)
            bankrd[tb * 2 + 1] = (ACT, ACT.n)

            @step("vector", waits=[(ACT, actf)], inc=VEC, inck=2)
            def _(E, link, fin, tb=tb, r=r):
                link(E.tensor_add(t_ffn, t_ffn, t_b2row))
                fin(E.tensor_add(t_v3, t_ffn, t_ln1save[r % 2][:, tb, :]))
            vv2 = VEC.n

            @step("scalar", waits=[(VEC, vv2)], inc=ACT, inck=2)
            def _(E, link, fin):
                link(E.activation(t_ffn, t_v3, AF.Copy,
                                  accum_out=col(t_stats, S_S1)))
                fin(E.activation(t_ffn, t_v3, AF.Square,
                                 accum_out=col(t_stats, S_S2)))
            actst2 = ACT.n

            @step("vector", waits=[(ACT, actst2)], inc=VEC, inck=3)
            def _(E, link, fin):
                link(E.tensor_scalar_mul(col(t_stats, S_MU),
                                         col(t_stats, S_S1), 1.0 / DM))
                link(E.tensor_mul(col(t_stats, S_VAR), col(t_stats, S_MU),
                                  col(t_stats, S_MU)))
                fin(E.scalar_tensor_tensor(col(t_stats, S_VAR),
                                           col(t_stats, S_S2), 1.0 / DM,
                                           col(t_stats, S_VAR),
                                           ALU.mult, ALU.subtract))
            vvar2 = VEC.n

            @step("scalar", waits=[(VEC, vvar2)], inc=ACT, inck=2)
            def _(E, link, fin):
                link(E.activation(col(t_stats, S_RSTD), col(t_stats, S_VAR),
                                  AF.Ln, bias=col(t_consts, C_EPS)))
                fin(E.activation(col(t_stats, S_RSTD), col(t_stats, S_RSTD),
                                 AF.Exp, scale=col(t_consts, C_M05)))
            actrs2 = ACT.n

            @step("vector", waits=[(ACT, actrs2)], inc=VEC)
            def _(E, link, fin):
                fin(E.tensor_scalar(t_scr, t_v3, col(t_stats, S_MU),
                                    col(t_stats, S_RSTD), ALU.subtract,
                                    ALU.mult))
            vout = VEC.n

            @step("sync", waits=[(VEC, vout)], inc=D_OUT, inck=16)
            def _(E, link, fin, r=r, tb=tb):
                r0 = r * SUB + tb * 128
                fin(E.dma_start(out=d_out[r0:r0 + 128, :], in_=t_scr))
            yield

    def chain(*gens):
        for g in gens:
            yield from g

    def drive(gens, bg=None, bg_cap=0):
        gens = [g for g in gens if g is not None]
        n_bg = 0
        while gens or (bg is not None and n_bg < bg_cap):
            alive = []
            for g in gens:
                try:
                    next(g)
                    alive.append(g)
                except StopIteration:
                    pass
            gens = alive
            if bg is not None and n_bg < bg_cap:
                n_bg += 1
                try:
                    next(bg)
                except StopIteration:
                    bg = None
        return bg

    gwp = emit_WPREP()
    for tau in range(RND + 3):
        r2 = tau - 1
        r3 = tau - 2
        g2 = chain(*[emit_A2(bt, r2, j, (j * B + bt) % 2)
                     for (bt, j) in [(0, 0), (1, 0), (0, 1), (1, 1)]]) \
            if 1 <= tau <= RND else None
        g1 = chain(emit_A1a(0, tau), emit_A1b(0, tau),
                   emit_A1a(1, tau), emit_A1b(1, tau)) \
            if tau < RND else None
        parts3 = []
        if 0 <= r3 <= RND - 1:
            parts3.append(emit_A3a(r3))
        if 0 <= r3 - 1 <= RND - 1:
            parts3.append(emit_A3c(r3 - 1))
        if 0 <= r3 <= RND - 1:
            parts3.append(emit_A3b(r3))
        g3 = chain(*parts3) if parts3 else None
        gwp = drive([g2, g1, g3], bg=gwp, bg_cap=80)
        if 1 <= tau <= RND:
            r = tau - 1
            w = [yg_stored[(bt, r, j)] for bt in range(B) for j in range(NJ2)]

            @step("gpsimd", waits=w, inc=CC)
            def _(E, link, fin, r=r):
                from concourse.aot_env import allow_ici_all_to_all
                row0 = r * DI
                with allow_ici_all_to_all():
                    fin(E.collective_compute(
                        "AllToAll", ALU.bypass,
                        replica_groups=[list(range(NCORE))],
                        ins=[d_yg[row0:row0 + DI, :]],
                        outs=[d_ygx[row0:row0 + DI, :]]))
            cc2_done[r] = CC.n

    final_out = D_OUT.n

    @step("sync", waits=[(D_OUT, final_out)])
    def _(E, link, fin):
        pass

    with nc.Block() as blk:
        @blk.sync
        def _(eng):
            engobj.update(sync=nc.sync, gpsimd=nc.gpsimd, scalar=nc.scalar,
                          vector=nc.vector, tensor=nc.tensor)
            for fn in prog["sync"]:
                fn()

        @blk.gpsimd
        def _(eng):
            for fn in prog["gpsimd"]:
                fn()

        @blk.scalar
        def _(eng):
            for fn in prog["scalar"]:
                fn()

        @blk.vector
        def _(eng):
            for fn in prog["vector"]:
                fn()

        @blk.tensor
        def _(eng):
            for fn in prog["tensor"]:
                fn()

    es.close()
    return nc


def _mk_in_maps(inp):
    TTOT, DSL, TSL = _dims()
    DH = DSL // 128
    RND = L // C1
    SUB = B * C1 // NCORE
    x = np.ascontiguousarray(
        np.asarray(inp["L0_attn_x"], dtype=np.float32).reshape(TTOT, DM))
    win = np.asarray(inp["L0_attn_in_proj_w"], dtype=np.float32)
    inb = np.asarray(inp["L0_attn_in_proj_b"], dtype=np.float32)
    convw = np.asarray(inp["L0_attn_conv_w"], dtype=np.float32)[:, 0, :]
    convb = np.asarray(inp["L0_attn_conv_b"], dtype=np.float32)
    xpw = np.asarray(inp["L0_attn_x_proj_w"], dtype=np.float32)
    dtw = np.asarray(inp["L0_attn_dt_proj_w"], dtype=np.float32)
    dtb = np.asarray(inp["L0_attn_dt_proj_b"], dtype=np.float32)
    alog = np.asarray(inp["L0_attn_A_log"], dtype=np.float32)
    dpar = np.asarray(inp["L0_attn_D_param"], dtype=np.float32)
    ow = np.asarray(inp["L0_attn_out_proj_w"], dtype=np.float32)
    ob = np.asarray(inp["L0_attn_out_proj_b"], dtype=np.float32)
    w1 = np.asarray(inp["L0_ffn_w1"], dtype=np.float32)
    b1 = np.asarray(inp["L0_ffn_b1"], dtype=np.float32)
    w2 = np.asarray(inp["L0_ffn_w2"], dtype=np.float32)
    b2 = np.asarray(inp["L0_ffn_b2"], dtype=np.float32)

    import ml_dtypes
    idf = np.eye(128, dtype=np.float32)
    idb = np.eye(128, dtype=np.float32).astype(ml_dtypes.bfloat16)

    def dh_split(v):
        v = np.asarray(v)
        return np.ascontiguousarray(
            v.reshape(DH, 128, -1).transpose(1, 0, 2).reshape(128, -1))

    maps = []
    for c in range(NCORE):
        sl = slice(c * DSL, (c + 1) * DSL)
        rows = np.r_[c * DSL:(c + 1) * DSL, DI + c * DSL:DI + (c + 1) * DSL]
        winT = np.ascontiguousarray(win[rows, :].T)
        bt = c // (NCORE // B)
        ji = c % (NCORE // B)
        xm = np.concatenate(
            [x[bt * L + r * C1 + ji * SUB: bt * L + r * C1 + ji * SUB + SUB, :]
             for r in range(RND)], axis=0)
        m = dict(
            x=x,
            xmine=np.ascontiguousarray(xm),
            winT=winT,
            inb=np.concatenate([dh_split(inb[sl]),
                                dh_split(inb[DI + c * DSL:DI + (c + 1) * DSL])],
                               axis=1),
            convw=dh_split(convw[sl]),
            convb=dh_split(convb[sl]),
            xpwT=dh_split(np.ascontiguousarray(xpw[:, sl].T)),
            dtwT=np.ascontiguousarray(dtw[sl, :].T),
            dtb=dh_split(dtb[sl]),
            alog=dh_split(alog[sl, :]),
            dpar=dh_split(dpar[sl]),
            owT=np.ascontiguousarray(ow.T),
            ob=ob.reshape(1, DM),
            w1T=np.ascontiguousarray(
                w1.reshape(HID // 128, 128, DM // 128, 128)
                .transpose(0, 3, 2, 1).reshape(HID, DM)),
            b1=np.ascontiguousarray(b1.reshape(HID // 128, 128).T),
            w2T=np.ascontiguousarray(w2.T),
            b2=b2.reshape(1, DM),
            idf=idf, idb=idb,
        )
        maps.append(m)
    return maps


def kernel(**inp):
    TTOT, DSL, TSL = _dims()
    RND = L // C1
    SUB = B * C1 // NCORE
    in_maps = _mk_in_maps(inp)
    nc = build_program()
    res = run_bass_kernel_spmd(nc, in_maps, core_ids=list(range(NCORE)))
    out = np.empty((TTOT, DM), dtype=np.float32)
    for c in range(NCORE):
        o = np.asarray(res.results[c]["out"], dtype=np.float32)
        bt = c // (NCORE // B)
        ji = c % (NCORE // B)
        for r in range(RND):
            t0 = bt * L + r * C1 + ji * SUB
            out[t0:t0 + SUB, :] = o[r * SUB:(r + 1) * SUB, :]
    return out.reshape(B, L, DM)
